# version 06
# speedup vs baseline: 1.1132x; 1.0390x over previous
import numpy as np
import ml_dtypes
from contextlib import ExitStack

import concourse.bass as bass
import concourse.mybir as mybir
from concourse.bass_types import AP
from concourse.bass_utils import run_bass_kernel_spmd

F32 = mybir.dt.float32
BF16 = mybir.dt.bfloat16
AF = mybir.ActivationFunctionType
ALU = mybir.AluOpType
AX = mybir.AxisListType

NCORE = 8
D = 4096
KT = 32
BB, SS = 4, 2048
NBF = np.dtype(ml_dtypes.bfloat16)

ENGINES = ("tensor", "vector", "scalar", "gpsimd", "sync")
QUEUES = ("qsync", "qscalar", "qpool", "qvector")
Q2E = {"qsync": "sync", "qscalar": "scalar", "qpool": "gpsimd", "qvector": "vector"}


class Prog:
    def __init__(self, nc, ctx):
        self.nc = nc
        self.ctx = ctx
        self.sem = {}
        self.cnt = {}
        for e in ENGINES:
            self.sem[e] = ctx.enter_context(nc.semaphore(f"S{e}"))
            self.cnt[e] = 0
        self.queue = {e: [] for e in ENGINES}
        self.last_write = {}
        self.readers = {}
        self.satisfied = {e: {} for e in ENGINES}
        self.dma_sem_of = {}
        self.n_dma_sems = 0
        self.n_waits = 0
        self.n_ops = dict.fromkeys(ENGINES, 0)
        self.n_dmas = 0

    def _need_tokens(self, reads, writes):
        need = {}
        def add(tok):
            if tok is not None and need.get(tok[0], 0) < tok[1]:
                need[tok[0]] = tok[1]
        for r in reads:
            add(self.last_write.get(r))
        for w in writes:
            add(self.last_write.get(w))
            for tok in self.readers.get(w, ()):
                add(tok)
        return need

    def _plan_waits(self, ename, need):
        sat = self.satisfied[ename]
        out = []
        for s, v in sorted(need.items()):
            if sat.get(s, 0) >= v:
                continue
            out.append((s, v))
            sat[s] = v
        return out

    def _register(self, sem_name, reads, writes, val=None):
        tok = (sem_name, self.cnt[sem_name] if val is None else val)
        for w in writes:
            self.last_write[w] = tok
            self.readers[w] = []
        for r in reads:
            self.readers.setdefault(r, []).append(tok)

    def _record(self, ename, waits, emit_fn, inc_sem, inc_val):
        sem = self.sem
        def closure(eng):
            for s, v in waits:
                eng.wait_ge(sem[s], v)
            ins = emit_fn(eng)
            ins.then_inc(sem[inc_sem], inc_val)
            return ins
        self.queue[ename].append(closure)
        self.n_waits += len(waits)

    DMA_SEM_CAP = 88

    def _dma_sem(self, sem_key):
        s = self.dma_sem_of.get(sem_key)
        if s is None:
            idx = self.n_dma_sems
            self.n_dma_sems += 1
            s = f"D{idx % self.DMA_SEM_CAP}"
            if idx < self.DMA_SEM_CAP:
                self.sem[s] = self.ctx.enter_context(self.nc.semaphore(s))
                self.cnt[s] = 0
            self.dma_sem_of[sem_key] = s
        return s

    def collective(self, kind, op, groups, ins, outs, reads=(),
                   writes=(), sem_key="cc"):
        ename = "gpsimd"
        s = self._dma_sem(sem_key)
        need = self._need_tokens(reads, writes)
        waits = self._plan_waits(ename, need)
        self.cnt[s] += 16
        final = self.cnt[s]
        sem = self.sem
        def closure(eng, waits=waits, s=s):
            for ws, v in waits:
                eng.wait_ge(sem[ws], v)
            return eng.collective_compute(
                kind, op, replica_groups=groups, ins=ins,
                outs=outs).then_inc(sem[s], 16)
        self.queue[ename].append(closure)
        self.n_waits += len(waits)
        self._register(s, list(reads), list(writes), val=final)

    def op(self, engine, emit_fn, reads=(), writes=()):
        assert engine in ENGINES, engine
        need = self._need_tokens(reads, writes)
        waits = self._plan_waits(engine, need)
        self.cnt[engine] += 1
        self._record(engine, waits, emit_fn, engine, 1)
        self._register(engine, reads, writes)
        self.n_ops[engine] += 1

    def dma(self, q, out, in_, reads=(), writes=(), sem_key=None, **kw):
        self.dma_group(q, [(out, in_, reads, writes)], sem_key=sem_key, **kw)

    def dma_group(self, q, transfers, sem_key=None, **kw):
        assert q in QUEUES, q
        ename = Q2E[q]
        if sem_key is None:
            _, _, r0, w0 = transfers[0]
            sem_key = (list(w0) + list(r0))[0]
        s = self._dma_sem(sem_key)
        all_reads, all_writes, need = [], [], {}
        for out, in_, reads, writes in transfers:
            for k, v in self._need_tokens(reads, writes).items():
                if need.get(k, 0) < v:
                    need[k] = v
            all_reads += list(reads)
            all_writes += list(writes)
        waits = self._plan_waits(ename, need)
        self.cnt[s] += 16 * len(transfers)
        final = self.cnt[s]
        sem = self.sem
        def closure(eng, transfers=transfers, waits=waits, s=s, kw=kw):
            last = None
            for ws, v in waits:
                eng.wait_ge(sem[ws], v)
            for out, in_, _, _ in transfers:
                last = eng.dma_start(out=out, in_=in_, **kw).then_inc(sem[s], 16)
            return last
        self.queue[ename].append(closure)
        self.n_waits += len(waits)
        self._register(s, all_reads, all_writes, val=final)
        self.n_dmas += len(transfers)

    def mm_group(self, emit_list, reads=(), writes=()):
        need = self._need_tokens(reads, writes)
        waits = self._plan_waits("tensor", need)
        self.cnt["tensor"] += 1
        sem = self.sem
        def closure(eng, emit_list=emit_list, waits=waits):
            for s, v in waits:
                eng.wait_ge(sem[s], v)
            last = None
            for fn in emit_list:
                last = fn(eng)
            last.then_inc(sem["tensor"], 1)
            return last
        self.queue["tensor"].append(closure)
        self.n_waits += len(waits)
        self._register("tensor", reads, writes)
        self.n_ops["tensor"] += len(emit_list)

    def barrier(self):
        for e in ENGINES:
            sat = self.satisfied[e]
            todo = [(s, v) for s, v in self.cnt.items()
                    if v and s != e and sat.get(s, 0) < v]
            sem = self.sem
            def closure(eng, todo=todo):
                for s, v in todo:
                    eng.wait_ge(sem[s], v)
            self.queue[e].append(closure)
            for s, v in todo:
                sat[s] = v

    def finish(self, engine="sync"):
        sem = self.sem
        sat = self.satisfied[engine]
        todo = [(s, v) for s, v in self.cnt.items()
                if v and s != engine and sat.get(s, 0) < v]
        def closure(eng):
            for s, v in todo:
                eng.wait_ge(sem[s], v)
        self.queue[engine].append(closure)

    def emit(self):
        block = self.ctx.enter_context(self.nc.Block())
        for ename, section in (("tensor", block.tensor), ("vector", block.vector),
                               ("scalar", block.scalar), ("gpsimd", block.gpsimd),
                               ("sync", block.sync)):
            closures = self.queue[ename]
            def run(eng, closures=closures):
                for c in closures:
                    c(eng)
            section(run)


def tile_lhsT(W, ofree=128):
    O, K = W.shape
    assert O % ofree == 0 and K % 128 == 0
    t = W.reshape(O // ofree, ofree, K // 128, 128)
    return np.ascontiguousarray(t.transpose(0, 3, 2, 1).astype(np.float32))


def relvar(got, ref):
    got = np.asarray(got, np.float64)
    ref = np.asarray(ref, np.float64)
    return ((got - ref) ** 2).mean() / max((ref ** 2).mean(), 1e-300)


def chunks(ntok, width=512):
    return [slice(c * width, min((c + 1) * width, ntok))
            for c in range((ntok + width - 1) // width)]


class Mem:
    def __init__(self, nc, ctx, pref=""):
        self.nc, self.ctx, self.pref = nc, ctx, pref
        self.seen = set()

    def sb(self, name, shape, dt=F32):
        name = self.pref + name
        assert name not in self.seen, name
        self.seen.add(name)
        return self.ctx.enter_context(self.nc.sbuf_tensor(name, shape, dt))

    def ps(self, name, shape, dt=F32):
        name = self.pref + name
        assert name not in self.seen, name
        self.seen.add(name)
        return self.ctx.enter_context(self.nc.psum_tensor(name, shape, dt))


class StackMem:
    def __init__(self, nc, ctx, headroom=0):
        self.nc, self.ctx = nc, ctx
        self.base = (nc.sbuf_base + 31) // 32 * 32
        self.limit = nc.sbuf_top - headroom
        self.off = self.base
        self.peak = self.base

    def reset(self):
        self.off = self.base

    def sb(self, name, shape, dt=F32):
        nbytes = shape[1] * (4 if dt == F32 else 2)
        off = (self.off + 31) // 32 * 32
        t = self.nc.alloc_sbuf_tensor_at(name, list(shape), dt, offset=off)
        self.off = off + nbytes
        self.peak = max(self.peak, self.off)
        assert self.off <= self.limit, (name, self.off, self.limit)
        return t

    def ps(self, name, shape, dt=F32):
        return self.ctx.enter_context(self.nc.psum_tensor(name, shape, dt))


def make_pool(m):
    pool = {f"P{i}": m.ps(f"P{i}", [128, 512], F32) for i in range(6)}
    pool["PT"] = m.ps("PT", [128, 256], BF16)
    pool["PT2"] = m.ps("PT2", [128, 256], BF16)
    return pool


STAGE_EMITTERS = {}


def ln_prescale(p, m, io, ntok, pr, pool, xin=None, ub=None, bufs=None):
    cs_l = chunks(ntok)
    nch = len(cs_l)
    assert nch <= 2
    if xin is None:
        xin = io["xin"]
    if bufs is None:
        bufs = {"fresh": True,
                "xf": [m.sb(f"{pr}xf{i}", [128, ntok]) for i in range(2)],
                "x2": m.sb(f"{pr}x2", [128, ntok], BF16),
                "onc": m.sb(f"{pr}onc", [128, 1], BF16),
                "onr": m.sb(f"{pr}onr", [1, 128], F32),
                "rows": m.sb(f"{pr}rows", [1, 4 * ntok]),
                "ab": m.sb(f"{pr}ab", [128, 2 * ntok]),
                "eps": m.sb(f"{pr}eps", [1, 1])}
    else:
        bufs = dict(bufs, fresh=False)
    xf, x2 = bufs["xf"], bufs["x2"]
    if ub is None:
        ub = m.sb(f"{pr}ub", [128, KT * ntok], BF16)
    onescol, onesrow = bufs["onc"], bufs["onr"]
    rows, ab = bufs["rows"], bufs["ab"]
    SL = [rows[0:1, i * ntok:(i + 1) * ntok] for i in range(4)]
    _slot = {"sx": 0, "sxx": 1, "t1": 2, "t2": 0, "t3": 3, "t4": 0,
             "a": 1, "b": 2}
    R = {n: SL[i] for n, i in _slot.items()}
    rk = {n: f"{pr}rSL{i}" for n, i in _slot.items()}

    if bufs["fresh"]:
        p.op("gpsimd", lambda g: g.memset(onescol[:, :], 1.0), writes=[pr + "onc"])
        p.op("gpsimd", lambda g: g.memset(onesrow[:, :], 1.0), writes=[pr + "onr"])

    ubkeys = [f"{pr}ub:{kt}" for kt in range(KT)]
    for kt in range(KT):
        buf, bk = xf[kt % 2], f"{pr}xf{kt % 2}"
        p.dma("qpool", buf[:, :], xin[128 * kt:128 * (kt + 1), :],
              writes=[bk], sem_key=bk)
        usl = ub[:, kt * ntok:(kt + 1) * ntok]
        p.op("vector", lambda v, usl=usl, buf=buf: v.tensor_copy(usl, buf[:, :]),
             reads=[bk], writes=[ubkeys[kt]])
        p.op("vector", lambda v, usl=usl: v.tensor_mul(x2[:, :], usl, usl),
             reads=[ubkeys[kt]], writes=[pr + "x2"])
        for c, cs in enumerate(cs_l):
            w = cs.stop - cs.start
            px, pxk = pool[f"P{c}"], f"P{c}"
            p.mm_group([lambda t, usl=usl, cs=cs, w=w, px=px,
                        st=(kt == 0), sp=(kt == KT - 1): t.matmul(
                px[0:1, 0:w], onescol[:, :], usl[:, cs], start=st, stop=sp)],
                reads=[pr + "onc", ubkeys[kt]], writes=[pxk])
            px2, px2k = pool[f"P{c + 2}"], f"P{c + 2}"
            p.mm_group([lambda t, cs=cs, w=w, px2=px2,
                        st=(kt == 0), sp=(kt == KT - 1): t.matmul(
                px2[0:1, 0:w], onescol[:, :], x2[:, cs], start=st, stop=sp)],
                reads=[pr + "onc", pr + "x2"], writes=[px2k])

    for c, cs in enumerate(cs_l):
        w = cs.stop - cs.start
        p.op("vector", lambda v, c=c, cs=cs, w=w: v.tensor_copy(
            R["sx"][:, cs], pool[f"P{c}"][0:1, 0:w]), reads=[f"P{c}"],
            writes=[rk["sx"]])
        p.op("vector", lambda v, c=c, cs=cs, w=w: v.tensor_copy(
            R["sxx"][:, cs], pool[f"P{c + 2}"][0:1, 0:w]), reads=[f"P{c + 2}"],
            writes=[rk["sxx"]])
    p.op("scalar", lambda s: s.mul(R["t1"], R["sx"], 1.0 / 4096.0),
         reads=[rk["sx"]], writes=[rk["t1"]])
    p.op("scalar", lambda s: s.mul(R["t2"], R["sxx"], 1.0 / 4096.0),
         reads=[rk["sxx"], rk["sx"]], writes=[rk["t2"]])
    p.op("vector", lambda v: v.tensor_mul(R["t3"], R["t1"], R["t1"]),
         reads=[rk["t1"]], writes=[rk["t3"]])
    p.op("vector", lambda v: v.tensor_sub(R["t4"], R["t2"], R["t3"]),
         reads=[rk["t2"], rk["t3"]], writes=[rk["t4"]])
    epsc = bufs["eps"]
    if bufs["fresh"]:
        p.op("gpsimd", lambda g: g.memset(epsc[:, :], 1e-5), writes=[pr + "eps"])
    p.op("scalar", lambda s: s.activation(R["t4"], R["t4"], AF.Sqrt,
         bias=epsc[0:1, 0:1]), reads=[rk["t4"], pr + "eps"], writes=[rk["t4"]])
    p.op("vector", lambda v: v.reciprocal(R["a"], R["t4"]),
         reads=[rk["t4"], rk["sxx"]], writes=[rk["a"]])
    p.op("vector", lambda v: v.tensor_mul(R["t3"], R["t1"], R["a"]),
         reads=[rk["t1"], rk["a"]], writes=[rk["t3"]])
    p.op("scalar", lambda s: s.mul(R["b"], R["t3"], -1.0),
         reads=[rk["t3"], rk["t1"]], writes=[rk["b"]])

    abkeys = []
    for half, key in ((0, "a"), (1, "b")):
        for c, cs in enumerate(cs_l):
            w = cs.stop - cs.start
            dst = ab[:, half * ntok:(half + 1) * ntok]
            p.mm_group([lambda t, key=key, cs=cs, w=w: t.matmul(
                pool["P4"][:, 0:w], onesrow[:, :], R[key][:, cs],
                start=True, stop=True)],
                reads=[pr + "onr", rk[key]], writes=["P4"])
            k = f"{pr}ab:{half}:{c}"
            abkeys.append(k)
            p.op("vector", lambda v, dst=dst, cs=cs, w=w: v.tensor_copy(
                dst[:, cs], pool["P4"][:, 0:w]), reads=["P4"], writes=[k])

    Arow, Brow = ab[:, 0:ntok], ab[:, ntok:2 * ntok]
    for kt in range(KT):
        usl = ub[:, kt * ntok:(kt + 1) * ntok]
        p.op("vector", lambda v, usl=usl: v.tensor_mul(usl, usl, Arow),
             reads=[ubkeys[kt]] + abkeys, writes=[ubkeys[kt]])
        p.op("vector", lambda v, usl=usl: v.tensor_add(usl, usl, Brow),
             reads=[ubkeys[kt]] + abkeys, writes=[ubkeys[kt]])
    return ub, R, rk, ubkeys, bufs


def xkv_half(io, kv2, ntok):
    xkv = io["xkv"]
    if xkv.tensor.shape[0] == D:
        return xkv[:, kv2 * ntok:(kv2 + 1) * ntok]
    return xkv[kv2 * D:(kv2 + 1) * D, :]


def residual_out(p, m, io, pr, o, xb, xbk, add_from, add_key, scale=1.0):
    p.dma("qsync", xb[:, :], io["xin"][128 * o:128 * (o + 1), :],
          writes=[xbk], sem_key=xbk)
    p.op("vector", lambda v: v.scalar_tensor_tensor(
        xb[:, :], add_from, scale, xb[:, :], ALU.mult, ALU.add),
        reads=[add_key, xbk], writes=[xbk])
    p.dma("qscalar", io["xout"][128 * o:128 * (o + 1), :], xb[:, :],
          reads=[xbk], sem_key=xbk + "st")


def emit_l0ffn(p, m, io, ntok, pool, pr="F"):
    cs_l = chunks(ntok)
    ub, R, rk, ubkeys, _lb = ln_prescale(p, m, io, ntok, pr, pool)

    wbuf = [m.sb(f"{pr}w{i}", [128, KT * 128], BF16) for i in range(2)]
    zt = m.sb(f"{pr}zt", [128, 4 * ntok], BF16)
    qrT = m.sb(f"{pr}qrT", [16, ntok], BF16)
    rw = m.sb(f"{pr}rw", [128, KT * 16], BF16)
    keys_sb = m.sb(f"{pr}keys", [16, 64], BF16)
    sc = m.sb(f"{pr}sc", [128, 64])
    msk = m.sb(f"{pr}msk", [128, 64])
    m8 = m.sb(f"{pr}m8", [128, 16])
    grow = m.sb(f"{pr}grow", [128, 2])
    gate = m.sb(f"{pr}gate", [128, 64], BF16)
    gateT = m.sb(f"{pr}gateT", [64, ntok], BF16)
    sel_sb = m.sb(f"{pr}sel", [64, 4 * 128], BF16)
    identb = m.sb(f"{pr}idb", [128, 128], BF16)
    xo = [m.sb(f"{pr}xo{i}", [128, ntok]) for i in range(2)]

    p.dma("qpool", keys_sb[:, :], io["l0f_keysT"], writes=[pr + "keys"])
    p.dma("qpool", sel_sb[:, :], io["l0f_sel"], writes=[pr + "sel"])
    p.dma("qpool", identb[:, :], io["identb"], writes=[pr + "idb"])
    p.dma("qpool", rw[:, :], io["l0f_WqT"].rearrange("o p k f -> p (o k f)"),
          writes=[pr + "rw"])

    for c, cs in enumerate(cs_l):
        w = cs.stop - cs.start
        px, pxk = pool[f"P{c}"], f"P{c}"
        mms = []
        for kt in range(KT):
            mms.append(lambda t, kt=kt, cs=cs, w=w, px=px: t.matmul(
                px[0:16, 0:w], rw[:, kt * 16:(kt + 1) * 16],
                ub[:, kt * ntok:(kt + 1) * ntok][:, cs],
                start=(kt == 0), stop=(kt == KT - 1)))
        p.mm_group(mms, reads=[pr + "rw"] + ubkeys, writes=[pxk])
        p.op("vector", lambda v, cs=cs, w=w, px=px: v.tensor_copy(
            qrT[:, cs], px[0:16, 0:w]), reads=[pxk], writes=[pr + "qrT"])

    for o in range(4):
        wb, wkey = wbuf[o % 2], f"{pr}w{o % 2}"
        p.dma("qpool", wb[:, :], io["l0f_AwT"][o].rearrange("p k f -> p (k f)"),
              writes=[wkey], sem_key=wkey)
        for c, cs in enumerate(cs_l):
            w = cs.stop - cs.start
            px, pxk = pool[f"P{2 + (c % 2)}"], f"P{2 + (c % 2)}"
            mms = []
            for kt in range(KT):
                mms.append(lambda t, kt=kt, cs=cs, w=w, px=px, wb=wb: t.matmul(
                    px[:, 0:w], wb[:, kt * 128:(kt + 1) * 128],
                    ub[:, kt * ntok:(kt + 1) * ntok][:, cs],
                    start=(kt == 0), stop=(kt == KT - 1)))
            p.mm_group(mms, reads=[wkey] + ubkeys, writes=[pxk])
            p.op("vector", lambda v, o=o, cs=cs, w=w, px=px: v.tensor_copy(
                zt[:, o * ntok:(o + 1) * ntok][:, cs], px[:, 0:w]),
                reads=[pxk], writes=[f"{pr}zt:{o}"])

    for tc in range(ntok // 128):
        ts = slice(tc * 128, (tc + 1) * 128)
        p.mm_group([lambda t, ts=ts: t.matmul(pool["P4"][:, 0:64], qrT[:, ts],
                                              keys_sb[:, :], start=True, stop=True)],
                   reads=[pr + "qrT", pr + "keys"], writes=["P4"])
        p.op("vector", lambda v: v.tensor_copy(sc[:, :], pool["P4"][:, 0:64]),
             reads=["P4"], writes=[pr + "sc"])
        p.op("vector", lambda v: v.max(out=m8[:, 0:8], in_=sc[:, :]),
             reads=[pr + "sc"], writes=[pr + "m8a"])
        p.op("vector", lambda v: v.match_replace(out=sc[:, :], in_to_replace=m8[:, 0:8],
             in_values=sc[:, :], imm_value=-1e30),
             reads=[pr + "sc", pr + "m8a"], writes=[pr + "sc"])
        p.op("vector", lambda v: v.max(out=m8[:, 8:16], in_=sc[:, :]),
             reads=[pr + "sc"], writes=[pr + "m8b"])
        p.op("vector", lambda v: v.tensor_copy(sc[:, :], pool["P4"][:, 0:64]),
             reads=["P4", pr + "sc"], writes=[pr + "sc"])
        p.op("scalar", lambda s: s.mul(grow[:, 0:1], m8[:, 0:1], -1.0),
             reads=[pr + "m8a"], writes=[pr + "negmax"])
        p.op("vector", lambda v: v.tensor_scalar(msk[:, :], sc[:, :], m8[:, 15:16],
             None, ALU.is_ge), reads=[pr + "sc", pr + "m8b"], writes=[pr + "msk"])
        p.op("scalar", lambda s: s.activation(sc[:, :], sc[:, :], AF.Exp,
             bias=grow[:, 0:1]), reads=[pr + "sc", pr + "negmax"], writes=[pr + "sc"])
        p.op("vector", lambda v: v.tensor_mul(sc[:, :], sc[:, :], msk[:, :]),
             reads=[pr + "sc", pr + "msk"], writes=[pr + "sc"])
        p.op("vector", lambda v: v.reduce_sum(grow[:, 1:2], sc[:, :], axis=AX.X),
             reads=[pr + "sc"], writes=[pr + "rsum"])
        p.op("vector", lambda v: v.reciprocal(grow[:, 1:2], grow[:, 1:2]),
             reads=[pr + "rsum"], writes=[pr + "rsum"])
        p.op("vector", lambda v: v.tensor_scalar_mul(gate[:, :], sc[:, :],
             grow[:, 1:2]), reads=[pr + "sc", pr + "rsum"], writes=[pr + "gate"])
        p.op("tensor", lambda t: t.transpose(pool["PT"][0:64, 0:128], gate[:, :],
             identb[:, :]), reads=[pr + "gate", pr + "idb"], writes=["PT"])
        p.op("vector", lambda v, ts=ts: v.tensor_copy(gateT[:, ts],
             pool["PT"][0:64, 0:128]), reads=["PT"], writes=[pr + "gateT"])

    for o in range(4):
        for c, cs in enumerate(cs_l):
            w = cs.stop - cs.start
            p.mm_group([lambda t, o=o, cs=cs, w=w: t.matmul(
                pool["P5"][:, 0:w], sel_sb[:, o * 128:(o + 1) * 128],
                gateT[:, cs], start=True, stop=True)],
                reads=[pr + "sel", pr + "gateT"], writes=["P5"])
            p.op("vector", lambda v, o=o, cs=cs, w=w: v.tensor_mul(
                zt[:, o * ntok:(o + 1) * ntok][:, cs],
                zt[:, o * ntok:(o + 1) * ntok][:, cs], pool["P5"][:, 0:w]),
                reads=[f"{pr}zt:{o}", "P5"], writes=[f"{pr}zt:{o}"])

    ztkeys = [f"{pr}zt:{i}" for i in range(4)]
    for o in range(KT):
        wb, wkey = wbuf[o % 2], f"{pr}w{o % 2}"
        p.dma("qpool", wb[:, 0:4 * 128], io["l0f_BwT"][o].rearrange("p k f -> p (k f)"),
              writes=[wkey], sem_key=wkey)
        xb, xbk = xo[o % 2], f"{pr}xo{o % 2}"
        p.dma("qsync", xb[:, :], io["xin"][128 * o:128 * (o + 1), :],
              writes=[xbk], sem_key=xbk)
        for c, cs in enumerate(cs_l):
            w = cs.stop - cs.start
            px, pxk = pool[f"P{c % 2}"], f"P{c % 2}"
            mms = []
            for i in range(4):
                mms.append(lambda t, i=i, cs=cs, w=w, px=px, wb=wb: t.matmul(
                    px[:, 0:w], wb[:, i * 128:(i + 1) * 128],
                    zt[:, i * ntok:(i + 1) * ntok][:, cs],
                    start=(i == 0), stop=(i == 3)))
            p.mm_group(mms, reads=[wkey] + ztkeys, writes=[pxk])
            p.op("vector", lambda v, cs=cs, w=w, px=px, xb=xb: v.scalar_tensor_tensor(
                xb[:, cs], px[:, 0:w], 2.0, xb[:, cs], ALU.mult, ALU.add),
                reads=[pxk, xbk], writes=[xbk])
        p.dma("qscalar", io["xout"][128 * o:128 * (o + 1), :], xb[:, :],
              reads=[xbk], sem_key=xbk + "st")


def prep_l0ffn(inp):
    ios = {}
    ios["l0f_AwT"] = tile_lhsT(inp["L0_ffn_A_w"])
    ios["l0f_BwT"] = tile_lhsT(inp["L0_ffn_B_w"])
    ios["l0f_WqT"] = tile_lhsT(inp["L0_ffn_Wq_w"], ofree=16)
    ios["l0f_keysT"] = np.ascontiguousarray(inp["L0_ffn_keys"].T.astype(np.float32))
    sel = np.zeros((64, 512), np.float32)
    for mcol in range(512):
        sel[mcol // 8, mcol] = 1.0
    ios["l0f_sel"] = sel
    return ios


def const_identb():
    return np.eye(128, dtype=np.float32)


def emit_l0aa(p, m, io, ntok, pool, pr="A"):
    nh = ntok // 2
    ub = m.sb(f"{pr}ub", [128, KT * nh], BF16)
    qsb = m.sb(f"{pr}qsb", [128, KT * nh], BF16)
    ksb = m.sb(f"{pr}ksb", [128, KT * nh], BF16)
    vsb = m.sb(f"{pr}vsb", [128, KT * nh], BF16)
    ctxsb = m.sb(f"{pr}ctx", [128, KT * nh], BF16)
    wbuf = m.sb(f"{pr}w", [128, KT * 128 + 128], BF16)
    Ssb = m.sb(f"{pr}Ssb", [128, 512])
    mx = m.sb(f"{pr}mx", [128, 32])
    Psb = m.sb(f"{pr}Psb", [128, 512], BF16)
    PTsb = m.sb(f"{pr}PTsb", [128, 512], BF16)
    VTsb = m.sb(f"{pr}VTsb", [128, 512], BF16)
    identb = m.sb(f"{pr}idb", [128, 128], BF16)
    gmask = m.sb(f"{pr}gm", [128, 256], BF16)
    bcol = m.sb(f"{pr}bcol", [128, 3 * 32])
    xo = [m.sb(f"{pr}xo{i}", [128, nh]) for i in range(2)]
    p.dma("qpool", identb[:, :], io["identb"], writes=[pr + "idb"])
    p.dma("qpool", gmask[:, :], io["l0a_gmask"], writes=[pr + "gm"])
    p.dma_group("qsync", [
        (bcol[:, 0:32], io["l0a_wqTb"], [], [pr + "bcol"]),
        (bcol[:, 32:64], io["l0a_wkTb"], [], [pr + "bcol"]),
        (bcol[:, 64:96], io["l0a_wvTb"], [], [pr + "bcol"]),
    ], sem_key=pr + "bcol")
    scale = float(1.0 / np.sqrt(128.0))

    for half in range(2):
        hp = f"{pr}h{half}"
        xin_sl = io["xin"][:, half * nh:(half + 1) * nh]
        lnbufs = None if half == 0 else lnbufs
        _, R, rk, ubkeys, lnbufs = ln_prescale(
            p, m, io, nh, f"{pr}LN", pool, xin=xin_sl, ub=ub, bufs=lnbufs)
        for wi, (wnm, dst, dstp) in enumerate(
                (("l0a_wqT", qsb, "q"), ("l0a_wkT", ksb, "k"),
                 ("l0a_wvT", vsb, "v"))):
            for o in range(KT):
                wkey = f"{pr}w"
                p.dma("qpool", wbuf[:, 0:KT * 128],
                      io[wnm][o].rearrange("p k f -> p (k f)"),
                      writes=[wkey], sem_key=wkey)
                px, pxk = pool[f"P{o % 2}"], f"P{o % 2}"
                mms = []
                for kt in range(KT):
                    mms.append(lambda t, kt=kt, px=px: t.matmul(
                        px[:, 0:nh], wbuf[:, kt * 128:(kt + 1) * 128],
                        ub[:, kt * nh:(kt + 1) * nh],
                        start=(kt == 0), stop=(kt == KT - 1)))
                p.mm_group(mms, reads=[wkey] + ubkeys, writes=[pxk])
                dk = f"{pr}{dstp}:{o}"
                p.op("vector", lambda v, o=o, px=px, dst=dst, wi=wi:
                     v.tensor_scalar_add(
                         AP(dst, o, [[KT * nh, 128], [32, nh]]), px[:, 0:nh],
                         bcol[:, wi * 32 + o:wi * 32 + o + 1]),
                     reads=[pxk, pr + "bcol"], writes=[dk])
        qkeys = [f"{pr}q:{o}" for o in range(KT)]
        kkeys = [f"{pr}k:{o}" for o in range(KT)]
        vkeys = [f"{pr}v:{o}" for o in range(KT)]

        for g8 in range(nh // 8):
            par = g8 % 2
            psS, psSk = pool[f"P{2 + par}"], f"P{2 + par}"
            psC, psCk = pool[f"P{4 + par}"], f"P{4 + par}"
            psV, psVk = (pool["PT"], "PT") if par == 0 else (pool["PT2"], "PT2")
            sK, pK, ptK, vtK, mxK, smK = (f"{pr}Ss{par}", f"{pr}Ps{par}",
                                          f"{pr}PT{par}", f"{pr}VT{par}",
                                          f"{pr}mx{par}", f"{pr}sm{par}")
            Ss = Ssb[:, par * 256:par * 256 + 256]
            Ps = Psb[:, par * 256:par * 256 + 256]
            PTs = PTsb[:, par * 256:par * 256 + 256]
            VTs = VTsb[:, par * 256:par * 256 + 256]
            mxn = mx[:, par * 16:par * 16 + 8]
            mxs = mx[:, par * 16 + 8:par * 16 + 16]
            for q4 in range(2):
                n0 = g8 * 8 + q4 * 4
                qpack = qsb[:, 32 * n0:32 * n0 + 128]
                vpack = vsb[:, 32 * n0:32 * n0 + 128]
                mms = []
                for j in range(4):
                    t8 = q4 * 4 + j
                    k3 = ksb[:, 32 * (n0 + j):32 * (n0 + j) + 32]
                    mms.append(lambda t, k3=k3, t8=t8, psS=psS, qpack=qpack:
                               t.matmul(psS[:, 32 * t8:32 * t8 + 32],
                                        qpack, k3, start=True, stop=True))
                p.mm_group(mms, reads=qkeys + kkeys, writes=[psSk])
                p.op("tensor", lambda t, vpack=vpack, q4=q4, psV=psV: t.transpose(
                    psV[:, 128 * q4:128 * q4 + 128], vpack, identb[:, :]),
                    reads=vkeys + [pr + "idb"], writes=[psVk])
            p.op("vector", lambda v, Ss=Ss, psS=psS: v.tensor_copy(
                Ss, psS[:, 0:256]), reads=[psSk], writes=[sK])
            p.op("vector", lambda v, Ss=Ss, mxn=mxn: v.reduce_max(
                mxn.rearrange("p (a one) -> p a one", one=1),
                Ss.rearrange("p (a b) -> p a b", b=32), axis=AX.X, negate=True),
                reads=[sK], writes=[mxK])
            p.op("vector", lambda v, Ss=Ss, mxn=mxn: v.tensor_tensor(
                Ss.rearrange("p (a b) -> p a b", b=32),
                Ss.rearrange("p (a b) -> p a b", b=32),
                mxn.unsqueeze(2).broadcast_to([128, 8, 32]), ALU.add),
                reads=[sK, mxK], writes=[sK])
            p.op("scalar", lambda s, Ss=Ss: s.activation(Ss, Ss, AF.Exp,
                 scale=scale), reads=[sK], writes=[sK])
            p.op("vector", lambda v, Ss=Ss, mxs=mxs: v.reduce_sum(
                mxs.rearrange("p (a one) -> p a one", one=1),
                Ss.rearrange("p (a b) -> p a b", b=32), axis=AX.X),
                reads=[sK], writes=[smK])
            p.op("vector", lambda v, mxs=mxs: v.reciprocal(mxs, mxs),
                 reads=[smK], writes=[smK])
            p.op("vector", lambda v, Ss=Ss, Ps=Ps, mxs=mxs: v.tensor_tensor(
                Ps.rearrange("p (a b) -> p a b", b=32),
                Ss.rearrange("p (a b) -> p a b", b=32),
                mxs.unsqueeze(2).broadcast_to([128, 8, 32]), ALU.mult),
                reads=[sK, smK], writes=[pK])
            p.op("vector", lambda v, Ps=Ps: v.tensor_mul(Ps, Ps, gmask[:, :]),
                 reads=[pK, pr + "gm"], writes=[pK])
            p.op("vector", lambda v, Ps=Ps, PTs=PTs: v.transpose(PTs, Ps),
                 reads=[pK], writes=[ptK])
            p.op("vector", lambda v, VTs=VTs, psV=psV: v.tensor_copy(
                VTs, psV[:, 0:256]), reads=[psVk], writes=[vtK])
            mms = []
            for t8 in range(8):
                i4 = t8 // 4
                mms.append(lambda t, i4=i4, t8=t8, VTs=VTs, PTs=PTs, psC=psC:
                           t.matmul(
                    psC[:, 32 * t8:32 * t8 + 32],
                    VTs[:, 128 * i4:128 * i4 + 128],
                    PTs[:, 32 * t8:32 * t8 + 32],
                    start=True, stop=True))
            p.mm_group(mms, reads=[ptK, vtK], writes=[psCk])
            dst = AP(ctxsb, g8 * 8, [[KT * nh, 128], [1, 8], [nh, 32]])
            p.op("vector", lambda v, dst=dst, psC=psC: v.tensor_copy(
                dst, psC[:, 0:256].rearrange("p (a b) -> p a b", b=32)),
                reads=[psCk], writes=[f"{pr}ctxw"])
        for o in range(KT):
            xb, xbk = xo[o % 2], f"{pr}xo{o % 2}"
            p.dma("qsync", xb[:, :], xin_sl[128 * o:128 * (o + 1), :],
                  writes=[xbk], sem_key=xbk)
            p.op("vector", lambda v, o=o, xb=xb: v.tensor_add(
                xb[:, :], xb[:, :], ctxsb[:, o * nh:(o + 1) * nh]),
                reads=[xbk, f"{pr}ctxw"], writes=[xbk])
            p.dma("qscalar",
                  io["xout"][128 * o:128 * (o + 1), half * nh:(half + 1) * nh],
                  xb[:, :], reads=[xbk], sem_key=xbk + "st")


def prep_l0aa(inp):
    ios = {}
    ios["l0a_wqT"] = tile_lhsT(inp["L0_attn_a_wq"].astype(np.float32))
    ios["l0a_wkT"] = tile_lhsT(inp["L0_attn_a_wk"].astype(np.float32))
    ios["l0a_wvT"] = tile_lhsT(inp["L0_attn_a_wv"].astype(np.float32))
    ios["l0a_wqTb"] = np.ascontiguousarray(
        inp["L0_attn_a_bq"].astype(np.float32).reshape(32, 128).T)
    ios["l0a_wkTb"] = np.ascontiguousarray(
        inp["L0_attn_a_bk"].astype(np.float32).reshape(32, 128).T)
    ios["l0a_wvTb"] = np.ascontiguousarray(
        inp["L0_attn_a_bv"].astype(np.float32).reshape(32, 128).T)
    gm = np.zeros((128, 256), np.float32)
    for c in range(8):
        j = c % 4
        gm[32 * j:32 * j + 32, 32 * c:32 * c + 32] = 1.0
    ios["l0a_gmask"] = gm
    return ios


STAGE_EMITTERS["l0aa"] = emit_l0aa


NHT = 88
NHS = 22


def emit_l1ffn(p, m, io, ntok, pool, pr="Q"):
    cs_l = chunks(ntok)
    ub, R, rk, ubkeys, _lb = ln_prescale(p, m, io, ntok, pr, pool)

    xsb = m.sb(f"{pr}xsb", [128, 8 * ntok], BF16)
    xst = m.sb(f"{pr}xst", [128, ntok], BF16)
    gbuf = m.sb(f"{pr}gbuf", [128, ntok])
    htile = m.sb(f"{pr}ht", [128, ntok], BF16)
    uarow = m.sb(f"{pr}uarow", [2, 3 * ntok], BF16)
    UA = {n: uarow[0:2, i * ntok:(i + 1) * ntok] for i, n in enumerate(
        ["ua1", "ua3", "ua2"])}
    uak = {n: f"{pr}ua_{n}" for n in UA}
    aw = m.sb(f"{pr}aw", [128, 3 * 64], BF16)
    awk = {"a1": f"{pr}aw_a1", "a3": f"{pr}aw_a3"}
    wb1 = [m.sb(f"{pr}wb1{i}", [128, 8 * 128 + 2 * 128], BF16) for i in range(2)]
    wb2 = [m.sb(f"{pr}wb2{i}", [128, NHS * 128 + 2 * 128], BF16) for i in range(2)]
    a2w = m.sb(f"{pr}a2w", [128, NHT * 2], BF16)
    xo = [m.sb(f"{pr}xo{i}", [128, ntok]) for i in range(2)]

    for i in range(8):
        xk = f"{pr}xs:{i}"
        p.op("vector", lambda v, i=i: v.tensor_add(
            xst[:, :], ub[:, i * ntok:(i + 1) * ntok],
            ub[:, (8 + i) * ntok:(9 + i) * ntok]),
            reads=[ubkeys[i], ubkeys[8 + i]], writes=[f"{pr}xst"])
        p.op("vector", lambda v, i=i: v.tensor_add(
            xsb[:, i * ntok:(i + 1) * ntok], ub[:, (16 + i) * ntok:(17 + i) * ntok],
            ub[:, (24 + i) * ntok:(25 + i) * ntok]),
            reads=[ubkeys[16 + i], ubkeys[24 + i]], writes=[xk])
        p.op("vector", lambda v, i=i: v.tensor_add(
            xsb[:, i * ntok:(i + 1) * ntok], xsb[:, i * ntok:(i + 1) * ntok],
            xst[:, :]), reads=[xk, f"{pr}xst"], writes=[xk])
    xskeys = [f"{pr}xs:{i}" for i in range(8)]

    p.dma("qpool", aw[:, 0:64], io["l1f_a1T"][0].rearrange("p k f -> p (k f)"),
          writes=[awk["a1"]], sem_key=awk["a1"])
    p.dma("qpool", aw[:, 64:128], io["l1f_a3T"][0].rearrange("p k f -> p (k f)"),
          writes=[awk["a3"]], sem_key=awk["a3"])
    p.dma("qpool", a2w[:, :], io["l1f_a2T"][0].rearrange("p k f -> p (k f)"),
          writes=[f"{pr}a2w"], sem_key=f"{pr}a2w")
    for nm, awkey, off in (("ua1", awk["a1"], 0), ("ua3", awk["a3"], 64)):
        for c, cs in enumerate(cs_l):
            w = cs.stop - cs.start
            px, pxk = pool[f"P{c}"], f"P{c}"
            mms = []
            for kt in range(KT):
                mms.append(lambda t, kt=kt, cs=cs, w=w, px=px, off=off,
                           st=(kt == 0), sp=(kt == KT - 1): t.matmul(
                    px[0:2, 0:w], aw[:, off + kt * 2:off + (kt + 1) * 2],
                    ub[:, kt * ntok:(kt + 1) * ntok][:, cs], start=st, stop=sp))
            p.mm_group(mms, reads=[awkey] + ubkeys, writes=[pxk])
            p.op("scalar", lambda s, nm=nm, cs=cs, w=w, px=px: s.mul(
                UA[nm][:, cs], px[0:2, 0:w], 0.5),
                reads=[pxk], writes=[uak[nm]])

    hsk = [f"{pr}hsb:{a}" for a in range(NHS)]
    for a in range(NHS):
        for j in range(4):
            o = 4 * a + j
            wb, wkey = wb1[o % 2], f"{pr}wb1{o % 2}"
            p.dma_group("qpool", [
                (wb[:, 0:8 * 128],
                 io["l1f_w1T"][o].rearrange("p k f -> p (k f)"), [], [wkey]),
                (wb[:, 8 * 128:8 * 128 + 128], io["l1f_b1T"][o], [], [wkey]),
                (wb[:, 8 * 128 + 128:8 * 128 + 256], io["l1f_b3T"][o], [], [wkey]),
            ], sem_key=wkey)
            wb3, wkey3 = wb2[o % 2], f"{pr}wb2{o % 2}"
            p.dma("qpool", wb3[:, 0:8 * 128],
                  io["l1f_w3T"][o].rearrange("p k f -> p (k f)"),
                  writes=[wkey3], sem_key=wkey3)
            for c, cs in enumerate(cs_l):
                w = cs.stop - cs.start
                px, pxk = pool[f"P{c}"], f"P{c}"
                mms = []
                for kt in range(8):
                    mms.append(lambda t, kt=kt, cs=cs, w=w, px=px, wb=wb: t.matmul(
                        px[:, 0:w], wb[:, kt * 128:(kt + 1) * 128],
                        xsb[:, kt * ntok:(kt + 1) * ntok][:, cs],
                        start=(kt == 0), stop=False))
                mms.append(lambda t, cs=cs, w=w, px=px, wb=wb: t.matmul(
                    px[:, 0:w], wb[0:2, 8 * 128:8 * 128 + 128],
                    UA["ua1"][:, cs], start=False, stop=True))
                p.mm_group(mms, reads=[wkey] + xskeys + [uak["ua1"]], writes=[pxk])
                p.op("scalar", lambda s, cs=cs, w=w, px=px: s.activation(
                    gbuf[:, cs], px[:, 0:w], AF.Silu),
                    reads=[pxk], writes=[f"{pr}gbuf"])
                px2, pxk2 = pool[f"P{2 + c}"], f"P{2 + c}"
                mms = []
                for kt in range(8):
                    mms.append(lambda t, kt=kt, cs=cs, w=w, px2=px2, wb3=wb3:
                               t.matmul(px2[:, 0:w],
                                        wb3[:, kt * 128:(kt + 1) * 128],
                                        xsb[:, kt * ntok:(kt + 1) * ntok][:, cs],
                                        start=(kt == 0), stop=False))
                mms.append(lambda t, cs=cs, w=w, px2=px2, wb=wb: t.matmul(
                    px2[:, 0:w], wb[0:2, 8 * 128 + 128:8 * 128 + 256],
                    UA["ua3"][:, cs], start=False, stop=True))
                p.mm_group(mms, reads=[wkey3, wkey] + xskeys + [uak["ua3"]],
                           writes=[pxk2])
                p.op("vector", lambda v, cs=cs, w=w, px2=px2: v.tensor_mul(
                    htile[:, cs], gbuf[:, cs], px2[:, 0:w]),
                    reads=[f"{pr}gbuf", pxk2], writes=[f"{pr}htile"])
            hs_sl = ub[:, a * ntok:(a + 1) * ntok]
            if j == 0:
                p.op("vector", lambda v, hs_sl=hs_sl: v.tensor_copy(
                    hs_sl, htile[:, :]),
                    reads=[f"{pr}htile", ubkeys[a]], writes=[hsk[a]])
            else:
                p.op("vector", lambda v, hs_sl=hs_sl: v.tensor_add(
                    hs_sl, hs_sl, htile[:, :]),
                    reads=[f"{pr}htile", hsk[a]], writes=[hsk[a]])
            for c, cs in enumerate(cs_l):
                w = cs.stop - cs.start
                px, pxk = pool[f"P{4 + c}"], f"P{4 + c}"
                p.mm_group([lambda t, o=o, cs=cs, w=w, px=px,
                            st=(o == 0), sp=(o == NHT - 1): t.matmul(
                    px[0:2, 0:w], a2w[:, o * 2:(o + 1) * 2], htile[:, cs],
                    start=st, stop=sp)],
                    reads=[f"{pr}a2w", f"{pr}htile"], writes=[pxk])
    for c, cs in enumerate(cs_l):
        w = cs.stop - cs.start
        p.op("scalar", lambda s, c=c, cs=cs, w=w: s.mul(
            UA["ua2"][:, cs], pool[f"P{4 + c}"][0:2, 0:w], 0.5),
            reads=[f"P{4 + c}"], writes=[uak["ua2"]])

    for o in range(KT):
        wb, wkey = wb2[o % 2], f"{pr}wb2{o % 2}"
        p.dma_group("qpool", [
            (wb[:, 0:NHS * 128], io["l1f_w2T"][o].rearrange("p k f -> p (k f)"),
             [], [wkey]),
            (wb[:, NHS * 128:NHS * 128 + 128], io["l1f_b2T"][o], [], [wkey]),
        ], sem_key=wkey)
        xb, xbk = xo[o % 2], f"{pr}xo{o % 2}"
        p.dma("qsync", xb[:, :], io["xin"][128 * o:128 * (o + 1), :],
              writes=[xbk], sem_key=xbk)
        for c, cs in enumerate(cs_l):
            w = cs.stop - cs.start
            px, pxk = pool[f"P{c}"], f"P{c}"
            mms = []
            for a in range(NHS):
                mms.append(lambda t, a=a, cs=cs, w=w, px=px, wb=wb: t.matmul(
                    px[:, 0:w], wb[:, a * 128:(a + 1) * 128],
                    ub[:, a * ntok:(a + 1) * ntok][:, cs],
                    start=(a == 0), stop=False))
            mms.append(lambda t, cs=cs, w=w, px=px, wb=wb: t.matmul(
                px[:, 0:w], wb[0:2, NHS * 128:NHS * 128 + 128],
                UA["ua2"][:, cs], start=False, stop=True))
            p.mm_group(mms, reads=[wkey] + hsk + [uak["ua2"]], writes=[pxk])
            p.op("vector", lambda v, cs=cs, w=w, px=px, xb=xb:
                 v.scalar_tensor_tensor(xb[:, cs], px[:, 0:w], 1.0, xb[:, cs],
                                        ALU.mult, ALU.add),
                 reads=[pxk, xbk], writes=[xbk])
        p.dma("qscalar", io["xout"][128 * o:128 * (o + 1), :], xb[:, :],
              reads=[xbk], sem_key=xbk + "st")


def prep_l1ffn(inp):
    ios = {}
    orig = np.arange(11008)
    j_ = orig // 2752
    mres = orig % 2752
    a_ = mres // 128
    r_ = mres % 128
    newpos = 512 * a_ + 128 * j_ + r_
    def permute_h_rows(Wfull):
        out = np.zeros((11264,) + Wfull.shape[1:], np.float32)
        out[newpos] = Wfull
        return out
    w1p = permute_h_rows(inp["L1_ffn_w1q"].astype(np.float32))
    w3p = permute_h_rows(inp["L1_ffn_w3q"].astype(np.float32))
    b1p = permute_h_rows(inp["L1_ffn_b1"].astype(np.float32))
    b3p = permute_h_rows(inp["L1_ffn_b3"].astype(np.float32))
    a2p = np.zeros((2, 11264), np.float32)
    a2p[:, newpos] = inp["L1_ffn_a2"].astype(np.float32)
    w2pad = np.zeros((4096, NHS * 128), np.float32)
    w2pad[:, 0:2752] = inp["L1_ffn_w2q"].astype(np.float32)
    ios["l1f_w1T"] = tile_lhsT(w1p)
    ios["l1f_w3T"] = tile_lhsT(w3p)
    ios["l1f_w2T"] = tile_lhsT(w2pad)
    ios["l1f_a1T"] = tile_lhsT(inp["L1_ffn_a1"].astype(np.float32), ofree=2)
    ios["l1f_a3T"] = tile_lhsT(inp["L1_ffn_a3"].astype(np.float32), ofree=2)
    ios["l1f_a2T"] = tile_lhsT(a2p, ofree=2)
    def btiles(bp, nt):
        out = np.zeros((nt, 128, 128), np.float32)
        out[:, 0:2, :] = bp.reshape(nt, 128, 2).transpose(0, 2, 1)
        return out
    ios["l1f_b1T"] = btiles(b1p, NHT)
    ios["l1f_b3T"] = btiles(b3p, NHT)
    ios["l1f_b2T"] = btiles(inp["L1_ffn_b2"].astype(np.float32), KT)
    return ios


STAGE_EMITTERS["l1ffn"] = emit_l1ffn


def rope_tile(p, m, pool, pr, psum, psumk, cosT, sinT, cos_cols, Rb, dst, dstk,
              scr, scrk):
    qf, qfk = scr["qf"], scrk + "qf"
    qb, qbk = scr["qb"], scrk + "qb"
    t1, t1k = scr["t1"], scrk + "t1"
    n = qf.shape[1]
    p.op("vector", lambda v: v.tensor_copy(qf[:, :], psum[:, 0:n]),
         reads=[psumk], writes=[qfk])
    p.op("vector", lambda v: v.tensor_copy(qb[:, :], psum[:, 0:n]),
         reads=[psumk], writes=[qbk])
    p.mm_group([lambda t: t.matmul(pool["P5"][:, 0:n], Rb[:, :], qb[:, :],
                                   start=True, stop=True)],
               reads=[qbk, "Rbconst"], writes=["P5"])
    p.op("vector", lambda v: v.tensor_mul(t1[:, :], qf[:, :],
         cosT[:, cos_cols]), reads=[qfk, "ropeconst"], writes=[t1k])
    p.op("vector", lambda v: v.tensor_mul(qf[:, :], pool["P5"][:, 0:n],
         sinT[:, cos_cols]), reads=["P5", "ropeconst"], writes=[qfk])
    p.op("vector", lambda v: v.tensor_add(dst, t1[:, :], qf[:, :]),
         reads=[t1k, qfk], writes=[dstk])


def emit_l1aa(p, m, io, ntok, pool, pr="G", half=0):
    nc = m.nc
    NKV = 2 * ntok
    nqb = ntok // 128
    nblk = NKV // 128
    scale = float(1.0 / np.sqrt(128.0))
    qrd = nc.dram_tensor(pr + "qrope", [D, ntok], BF16).ap()

    ub = m.sb(f"{pr}ub", [128, KT * ntok], BF16)
    kT = m.sb(f"{pr}kT", [128, 8 * NKV], BF16)
    vtm = m.sb(f"{pr}vtm", [128, 8 * NKV], BF16)
    wbuf = m.sb(f"{pr}w", [128, KT * 128], BF16)
    Rb = m.sb(f"{pr}Rb", [128, 128], BF16)
    identb = m.sb(f"{pr}idb", [128, 128], BF16)
    cosT = m.sb(f"{pr}cosT", [128, NKV], BF16)
    sinT = m.sb(f"{pr}sinT", [128, NKV], BF16)
    trimask = m.sb(f"{pr}tri", [128, 128], BF16)
    km2 = m.sb(f"{pr}km2", [128, 256], BF16)
    bm = m.sb(f"{pr}bm", [128, 128])
    qcosT = m.sb(f"{pr}qcos", [128, ntok], BF16)
    qsinT = m.sb(f"{pr}qsin", [128, ntok], BF16)
    scW = min(512, ntok)
    scr = {"qf": m.sb(f"{pr}qf", [128, scW]),
           "qb": m.sb(f"{pr}qb2", [128, scW], BF16),
           "t1": m.sb(f"{pr}t1", [128, scW], BF16)}
    qh = [m.sb(f"{pr}qh{i}", [128, 512], BF16) for i in range(2)]
    ET = m.sb(f"{pr}ET", [128, 512], BF16)
    vb = m.sb(f"{pr}vb", [128, ntok], BF16)
    mrow = m.sb(f"{pr}mrow", [1, 512], BF16)
    onesr = m.sb(f"{pr}onesr", [1, 128])
    onesrb = m.sb(f"{pr}onesrb", [1, 128], BF16)
    onesc = m.sb(f"{pr}onesc", [128, 1], BF16)
    mcol = m.sb(f"{pr}mcol", [128, 2])
    rrow = m.sb(f"{pr}rrow", [1, 512])
    rbs = m.sb(f"{pr}rbs", [128, 512])
    xo = [m.sb(f"{pr}xo{i}", [128, ntok]) for i in range(2)]

    p.dma("qpool", Rb[:, :], io["l1a_R"], writes=["Rbconst"])
    p.dma("qpool", identb[:, :], io["identb"], writes=[pr + "idb"])
    p.dma_group("qpool", [
        (cosT[:, :], io["l1a_cos"], [], ["ropeconst"]),
        (sinT[:, :], io["l1a_sin"], [], ["ropeconst"]),
        (qcosT[:, :], io["l1a_qcos"], [], ["ropeconst"]),
        (qsinT[:, :], io["l1a_qsin"], [], ["ropeconst"]),
    ], sem_key=pr + "cs")
    p.dma_group("qpool", [
        (trimask[:, :], io["l1a_tri"], [], [pr + "masks"]),
        (km2[:, :], io["l1a_km2"], [], [pr + "masks"]),
        (bm[:, :], io["l1a_bm"], [], [pr + "masks"]),
    ], sem_key=pr + "cs2")
    p.op("gpsimd", lambda g: g.memset(onesr[:, :], 1.0), writes=[pr + "onesr"])
    p.op("gpsimd", lambda g: g.memset(onesrb[:, :], 1.0), writes=[pr + "onesrb"])
    p.op("gpsimd", lambda g: g.memset(onesc[:, :], 1.0), writes=[pr + "onesc"])

    _, R, rk, ubkeys, lnb = ln_prescale(p, m, io, ntok, f"{pr}LN", pool, ub=ub)
    for o in range(KT):
        wkey = f"{pr}w"
        p.dma("qpool", wbuf[:, :], io["l1a_WqT"][o].rearrange("p k f -> p (k f)"),
              writes=[wkey], sem_key=wkey)
        for c, cs in enumerate(chunks(ntok)):
            w = cs.stop - cs.start
            px, pxk = pool[f"P{c % 2}"], f"P{c % 2}"
            mms = [lambda t, kt=kt, cs=cs, w=w, px=px: t.matmul(
                px[:, 0:w], wbuf[:, kt * 128:(kt + 1) * 128],
                ub[:, kt * ntok:(kt + 1) * ntok][:, cs],
                start=(kt == 0), stop=(kt == KT - 1)) for kt in range(KT)]
            p.mm_group(mms, reads=[wkey] + ubkeys, writes=[pxk])
            sub = {"qf": scr["qf"][:, 0:w], "qb": scr["qb"][:, 0:w],
                   "t1": scr["t1"][:, 0:w]}
            rope_tile(p, m, pool, pr, px, pxk, qcosT, qsinT, cs, Rb,
                      scr["qf"][:, 0:w], f"{pr}LNqf", sub, f"{pr}LN")
            p.op("vector", lambda v, w=w: v.tensor_copy(scr["qb"][:, 0:w],
                 scr["qf"][:, 0:w]), reads=[f"{pr}LNqf"], writes=[f"{pr}LNqb"])
            p.dma("qscalar", qrd[o * 128:(o + 1) * 128, cs], scr["qb"][:, 0:w],
                  reads=[f"{pr}LNqb"], writes=[f"{pr}qrd:{o}:{c}"],
                  sem_key=pr + "qrdst")

    for kv2 in range(2):
        _, R, rk, ubkeys, lnb = ln_prescale(
            p, m, io, ntok, f"{pr}LN", pool,
            xin=xkv_half(io, kv2, ntok), ub=ub, bufs=lnb)
        for o in range(8):
            wkey = f"{pr}w"
            p.dma("qpool", wbuf[:, :],
                  io["l1a_WkT"][o].rearrange("p k f -> p (k f)"),
                  writes=[wkey], sem_key=wkey)
            for c, cs in enumerate(chunks(ntok)):
                w = cs.stop - cs.start
                px, pxk = pool[f"P{c % 2}"], f"P{c % 2}"
                mms = [lambda t, kt=kt, cs=cs, w=w, px=px: t.matmul(
                    px[:, 0:w], wbuf[:, kt * 128:(kt + 1) * 128],
                    ub[:, kt * ntok:(kt + 1) * ntok][:, cs],
                    start=(kt == 0), stop=(kt == KT - 1)) for kt in range(KT)]
                p.mm_group(mms, reads=[wkey] + ubkeys, writes=[pxk])
                sub = {"qf": scr["qf"][:, 0:w], "qb": scr["qb"][:, 0:w],
                       "t1": scr["t1"][:, 0:w]}
                gcols = slice(kv2 * ntok + cs.start, kv2 * ntok + cs.stop)
                dstsl = kT[:, o * NKV + kv2 * ntok:o * NKV + (kv2 + 1) * ntok][:, cs]
                rope_tile(p, m, pool, pr, px, pxk, cosT, sinT, gcols, Rb,
                          dstsl, f"{pr}kT:{o}", sub, f"{pr}LN")
        for o in range(8):
            wkey = f"{pr}w"
            p.dma("qpool", wbuf[:, :],
                  io["l1a_WvT"][o].rearrange("p k f -> p (k f)"),
                  writes=[wkey], sem_key=wkey)
            for c, cs in enumerate(chunks(ntok)):
                w = cs.stop - cs.start
                px, pxk = pool[f"P{c % 2}"], f"P{c % 2}"
                mms = [lambda t, kt=kt, cs=cs, w=w, px=px: t.matmul(
                    px[:, 0:w], wbuf[:, kt * 128:(kt + 1) * 128],
                    ub[:, kt * ntok:(kt + 1) * ntok][:, cs],
                    start=(kt == 0), stop=(kt == KT - 1)) for kt in range(KT)]
                p.mm_group(mms, reads=[wkey] + ubkeys, writes=[pxk])
                p.op("vector", lambda v, cs=cs, w=w, px=px: v.tensor_copy(
                    vb[:, cs], px[:, 0:w]), reads=[pxk], writes=[f"{pr}vb"])
            for c8 in range(ntok // 128):
                gkb = kv2 * (ntok // 128) + c8
                pv, pvk = (pool["PT"], "PT") if c8 % 2 == 0 else (pool["PT2"], "PT2")
                p.op("tensor", lambda t, c8=c8, pv=pv: t.transpose(
                    pv[:, 0:128], vb[:, c8 * 128:(c8 + 1) * 128], identb[:, :]),
                    reads=[f"{pr}vb", pr + "idb"], writes=[pvk])
                p.op("vector", lambda v, o=o, gkb=gkb, pv=pv: v.tensor_copy(
                    vtm[:, (o * nblk + gkb) * 128:(o * nblk + gkb + 1) * 128],
                    pv[:, 0:128]), reads=[pvk], writes=[f"{pr}vtm:{o}:{gkb}"])

    kTkeys = [f"{pr}kT:{o}" for o in range(8)]
    qcW = min(512, ntok)
    nj4 = qcW // 128
    qrdkeys = [f"{pr}qrd:{o}:{c}" for o in range(KT)
               for c in range(len(chunks(ntok)))]
    for h in range(32):
        kvh = h // 4
        for qc2 in range(ntok // qcW):
            qbuf, qbk = qh[qc2 % 2], f"{pr}qh{qc2 % 2}"
            p.dma("qsync", qbuf[:, 0:qcW],
                  qrd[h * 128:(h + 1) * 128, qc2 * qcW:(qc2 + 1) * qcW],
                  reads=qrdkeys, writes=[qbk], sem_key=qbk)
            qb0 = nqb + qc2 * nj4
            gq_max = qb0 + nj4 - 1
            nkb = gq_max + 1
            for j4 in range(nj4):
                mc = mcol[:, 0:1]
                for kc, kcs in enumerate(chunks(nkb * 128)):
                    w = kcs.stop - kcs.start
                    px, pxk = pool[f"P{kc % 2}"], f"P{kc % 2}"
                    p.mm_group([lambda t, j4=j4, kcs=kcs, w=w, px=px, qbuf=qbuf,
                                kvh=kvh: t.matmul(
                        px[:, 0:w], qbuf[:, j4 * 128:(j4 + 1) * 128],
                        kT[:, kvh * NKV:(kvh + 1) * NKV][:, kcs],
                        start=True, stop=True)],
                        reads=[qbk] + kTkeys, writes=[pxk])
                    dgb = qb0 + j4
                    if kcs.start <= dgb * 128 < kcs.stop:
                        off = dgb * 128 - kcs.start
                        p.op("vector", lambda v, px=px, off=off: v.tensor_add(
                            px[:, off:off + 128], px[:, off:off + 128],
                            trimask[:, :]), reads=[pxk, pr + "masks"],
                            writes=[pxk])
                        w2 = w - (off + 128)
                        if w2 > 0:
                            p.op("vector", lambda v, px=px, off=off, w2=w2:
                                 v.tensor_scalar(px[:, off + 128:off + 128 + w2],
                                                 px[:, off + 128:off + 128 + w2],
                                                 -1e9, None, ALU.add),
                                 reads=[pxk], writes=[pxk])
                    dst = mcol[:, 0:1] if kc == 0 else mcol[:, 1:2]
                    dk = f"{pr}mc{0 if kc == 0 else 1}"
                    p.op("vector", lambda v, px=px, w=w, dst=dst: v.reduce_max(
                        dst, px[:, 0:w], axis=AX.X), reads=[pxk], writes=[dk])
                    if kc > 0:
                        p.op("vector", lambda v: v.tensor_max(
                            mcol[:, 0:1], mcol[:, 0:1], mcol[:, 1:2]),
                            reads=[f"{pr}mc0", f"{pr}mc1"], writes=[f"{pr}mc0"])
                p.op("vector", lambda v: v.tensor_scalar(
                    mcol[:, 1:2], mcol[:, 0:1], -1.0, None, ALU.mult),
                    reads=[f"{pr}mc0"], writes=[f"{pr}mc1"])
                p.op("vector", lambda v: v.tensor_copy(
                    scr["qb"][:, 0:1], mcol[:, 1:2]),
                    reads=[f"{pr}mc1"], writes=[f"{pr}LNqb"])
                p.op("tensor", lambda t: t.transpose(
                    pool["PT"][0:1, 0:128], scr["qb"][:, 0:1], identb[:, :]),
                    reads=[f"{pr}LNqb", pr + "idb"], writes=["PT"])
                p.op("vector", lambda v, j4=j4: v.tensor_copy(
                    mrow[0:1, j4 * 128:(j4 + 1) * 128],
                    pool["PT"][0:1, 0:128]),
                    reads=["PT"], writes=[f"{pr}mrow"])
            for kb in range(nkb):
                px, pxk = pool[f"P{2 + kb % 2}"], f"P{2 + kb % 2}"
                p.mm_group([
                    lambda t, kb=kb, kvh=kvh, px=px, qbuf=qbuf:
                    t.matmul(px[:, 0:qcW],
                             kT[:, kvh * NKV + kb * 128:kvh * NKV + (kb + 1) * 128],
                             qbuf[:, 0:qcW],
                             start=True, stop=False),
                    lambda t, px=px: t.matmul(
                        px[:, 0:qcW], onesrb[:, :], mrow[0:1, 0:qcW],
                        start=False, stop=True)],
                    reads=[qbk, f"{pr}mrow", pr + "onesrb"] + kTkeys,
                    writes=[pxk])
                for j4 in range(nj4):
                    dgb = qb0 + j4
                    jsl = slice(j4 * 128, (j4 + 1) * 128)
                    if kb == dgb:
                        p.op("vector", lambda v, px=px, jsl=jsl: v.tensor_add(
                            px[:, jsl], px[:, jsl], km2[:, 128:256]),
                            reads=[pxk, pr + "masks"], writes=[pxk])
                    if kb == dgb - nqb:
                        p.op("vector", lambda v, px=px, jsl=jsl: v.tensor_add(
                            px[:, jsl], px[:, jsl], km2[:, 0:128]),
                            reads=[pxk, pr + "masks"], writes=[pxk])
                    bmi = (qc2 * nj4 + j4) * 16 + kb
                    p.op("scalar", lambda s, px=px, jsl=jsl, bmi=bmi:
                         s.activation(ET[:, jsl], px[:, jsl], AF.Exp,
                                      scale=scale, bias=bm[:, bmi:bmi + 1]),
                         reads=[pxk, pr + "masks"], writes=[f"{pr}ET"])
                p.mm_group([lambda t, kvh=kvh, kb=kb,
                            st=(kb == 0), sp=(kb == nkb - 1): t.matmul(
                    pool["P4"][:, 0:qcW],
                    vtm[:, (kvh * nblk + kb) * 128:(kvh * nblk + kb + 1) * 128],
                    ET[:, 0:qcW], start=st, stop=sp)],
                    reads=[f"{pr}ET", f"{pr}vtm:{kvh}:{kb}"], writes=["P4m"])
                p.mm_group([lambda t, st=(kb == 0), sp=(kb == nkb - 1):
                            t.matmul(
                    pool["P5"][0:1, 0:qcW], onesc[:, :],
                    ET[:, 0:qcW], start=st, stop=sp)],
                    reads=[f"{pr}ET", pr + "onesc"], writes=["P5m"])
            p.op("vector", lambda v: v.tensor_copy(rrow[0:1, 0:qcW],
                 pool["P5"][0:1, 0:qcW]),
                 reads=["P5m"], writes=[f"{pr}rrow"])
            p.op("vector", lambda v: v.reciprocal(rrow[0:1, 0:qcW],
                 rrow[0:1, 0:qcW]), reads=[f"{pr}rrow"], writes=[f"{pr}rrow"])
            p.mm_group([lambda t: t.matmul(pool["P0"][:, 0:qcW], onesr[:, :],
                                           rrow[0:1, 0:qcW], start=True,
                                           stop=True)],
                       reads=[f"{pr}rrow", pr + "onesr"], writes=["P0"])
            p.op("vector", lambda v: v.tensor_copy(rbs[:, 0:qcW],
                 pool["P0"][:, 0:qcW]), reads=["P0"], writes=[f"{pr}rbs"])
            ck = f"{pr}ctx:{h}:{qc2}"
            p.op("vector", lambda v, h=h, qc2=qc2: v.tensor_mul(
                ub[:, h * ntok + qc2 * qcW:h * ntok + (qc2 + 1) * qcW],
                pool["P4"][:, 0:qcW], rbs[:, 0:qcW]),
                reads=["P4m", f"{pr}rbs"],
                writes=[ck, ubkeys[h]])

    ctxkeys = [f"{pr}ctx:{h}:{qc2}" for h in range(32)
               for qc2 in range(ntok // qcW)]
    for o in range(KT):
        wkey = f"{pr}w"
        p.dma("qpool", wbuf[:, :], io["l1a_WoT"][o].rearrange("p k f -> p (k f)"),
              writes=[wkey], sem_key=wkey)
        xb, xbk = xo[o % 2], f"{pr}xo{o % 2}"
        p.dma("qsync", xb[:, :], io["xin"][128 * o:128 * (o + 1), :],
              writes=[xbk], sem_key=xbk)
        for c, cs in enumerate(chunks(ntok)):
            w = cs.stop - cs.start
            px, pxk = pool[f"P{c % 2}"], f"P{c % 2}"
            mms = [lambda t, kt=kt, cs=cs, w=w, px=px: t.matmul(
                px[:, 0:w], wbuf[:, kt * 128:(kt + 1) * 128],
                ub[:, kt * ntok:(kt + 1) * ntok][:, cs],
                start=(kt == 0), stop=(kt == KT - 1)) for kt in range(KT)]
            p.mm_group(mms, reads=[wkey] + ctxkeys, writes=[pxk])
            p.op("vector", lambda v, cs=cs, w=w, px=px, xb=xb:
                 v.scalar_tensor_tensor(xb[:, cs], px[:, 0:w], 1.0, xb[:, cs],
                                        ALU.mult, ALU.add),
                 reads=[pxk, xbk], writes=[xbk])
        p.dma("qscalar", io["xout"][128 * o:128 * (o + 1), :], xb[:, :],
              reads=[xbk], sem_key=xbk + "st")


def prep_l1aa(inp, half, ntok=1024):
    ios = {}
    ios["l1a_WqT"] = tile_lhsT(inp["L1_attn_a_Wq"].astype(np.float32))
    ios["l1a_WkT"] = tile_lhsT(inp["L1_attn_a_Wk"].astype(np.float32))
    ios["l1a_WvT"] = tile_lhsT(inp["L1_attn_a_Wv"].astype(np.float32))
    ios["l1a_WoT"] = tile_lhsT(inp["L1_attn_a_Wo"].astype(np.float32))
    S = SS
    pos = np.arange(S, dtype=np.float64)[:, None]
    freqs = np.exp(-np.arange(0, 128, 2, dtype=np.float64) * (np.log(10000.0) / 128))
    ang = pos * freqs
    cos = np.concatenate([np.cos(ang), np.cos(ang)], -1)
    sin = np.concatenate([np.sin(ang), np.sin(ang)], -1)
    ios["l1a_cos"] = np.ascontiguousarray(cos.T.astype(np.float32))
    ios["l1a_sin"] = np.ascontiguousarray(sin.T.astype(np.float32))
    Rm = np.zeros((128, 128), np.float32)
    for mm_ in range(64):
        Rm[mm_ + 64, mm_] = -1.0
        Rm[mm_, mm_ + 64] = 1.0
    ios["l1a_R"] = Rm
    tri = np.where(np.arange(128)[None, :] > np.arange(128)[:, None],
                   np.float32(-1e9), np.float32(0.0))
    ios["l1a_tri"] = tri
    triT = np.where(np.arange(128)[:, None] > np.arange(128)[None, :],
                    np.float32(-1e9), np.float32(0.0))
    km2 = np.zeros((128, 256), np.float32)
    if half == 0:
        km2[:, 0:128] = triT
        km2[:, 128:256] = 0.0
    else:
        km2[:, 128:256] = triT
    ios["l1a_km2"] = km2
    nqb = ntok // 128
    nj4 = min(512, ntok) // 128
    bmv = np.zeros((128, 128), np.float32)
    for qc2 in range(ntok // min(512, ntok)):
        for j4 in range(nj4):
            dgb = nqb + qc2 * nj4 + j4
            own = (half * nqb) + qc2 * nj4 + j4
            for kb in range(nqb + qc2 * nj4 + nj4):
                vis = (kb < own)
                dia = (kb == own)
                bmv[:, (qc2 * nj4 + j4) * 16 + kb] =                     0.0 if (vis or dia) else -1e9
    ios["l1a_bm"] = bmv * np.float32(1.0 / np.sqrt(128.0))
    ios["l1a_qcos"] = np.ascontiguousarray(
        ios["l1a_cos"][:, half * ntok:half * ntok + ntok])
    ios["l1a_qsin"] = np.ascontiguousarray(
        ios["l1a_sin"][:, half * ntok:half * ntok + ntok])
    return ios


STAGE_EMITTERS["l1aa_h0"] = lambda p, m, io, ntok, pool: emit_l1aa(
    p, m, io, ntok, pool, half=0)
STAGE_EMITTERS["l1aa_h1"] = lambda p, m, io, ntok, pool: emit_l1aa(
    p, m, io, ntok, pool, half=1)


def emit_l1ab(p, m, io, ntok, pool, pr="N", half=0):
    nc = m.nc
    NKV = 2 * ntok
    nqb = ntok // 128
    nkb = NKV // 128
    ncb = NKV // 16
    ncc = min(ncb, 128)
    niw = ncc - 1
    nbl = ntok // 16
    scale = float(1.0 / np.sqrt(128.0))
    qoff = half * ntok
    qcW = min(512, ntok)
    nqc = ntok // qcW
    nj4 = qcW // 128
    ctxd = nc.dram_tensor(pr + "ctxd", [D, ntok], BF16).ap()
    qrawd = nc.dram_tensor(pr + "qrawd", [D, ntok], BF16).ap()
    qroped = nc.dram_tensor(pr + "qroped", [D, ntok], BF16).ap()
    gated = nc.dram_tensor(pr + "gated", [96, ntok], F32).ap()
    krd = nc.dram_tensor(pr + "krd", [8, nkb, 128, 128], BF16).ap()
    vtd = nc.dram_tensor(pr + "vtd", [8, nkb, 128, 128], BF16).ap()
    cvd = nc.dram_tensor(pr + "cvd", [8, 128, 128], BF16).ap()

    aliasW = (2 * nkb * 128 + nkb * qcW + nqb * 512 + 1024 + 2 * qcW + NKV
              + nqb * ncc + nqb * 128 + 512 + 256 + qcW + 512
              + max(niw, 1) + 128)
    ubW = max(KT * ntok, aliasW)
    ub = m.sb(f"{pr}ub", [128, ubW], BF16)
    bufA = m.sb(f"{pr}bufA", [128, 16 * 512], BF16)
    bufB = m.sb(f"{pr}bufB", [128, 16 * 512], BF16)
    ckT = m.sb(f"{pr}ckT", [128, 8 * ncc], BF16)
    cvtm = m.sb(f"{pr}cvtm", [128, 8 * 128], BF16)
    wbuf = m.sb(f"{pr}w", [128, KT * 128], BF16)
    cosT = m.sb(f"{pr}cosT", [128, NKV], BF16)
    sinT = m.sb(f"{pr}sinT", [128, NKV], BF16)
    qcosT = m.sb(f"{pr}qcosT", [128, ntok], BF16)
    qsinT = m.sb(f"{pr}qsinT", [128, ntok], BF16)
    Rb = m.sb(f"{pr}Rb", [128, 128], BF16)
    identb = m.sb(f"{pr}idb", [128, 128], BF16)
    scW = min(512, ntok)
    scr = {"qf": m.sb(f"{pr}qf", [128, scW]),
           "qb": m.sb(f"{pr}qb2", [128, scW], BF16),
           "t1": m.sb(f"{pr}t1", [128, scW], BF16)}
    kpc = m.sb(f"{pr}kpc", [128, 2 * 128])
    bcols = m.sb(f"{pr}bcols", [128, 4 * 16])
    gbias = m.sb(f"{pr}gbias", [96, 1])
    memc = m.sb(f"{pr}memc", [128, 16])
    uo = 0
    def ubc(w):
        nonlocal uo
        v = ub[:, uo:uo + w]
        uo += w
        return v
    krb = ubc(nkb * 128)
    vtb = ubc(nkb * 128)
    cntjT = ubc(nkb * qcW)
    cob_off = uo
    cob = ubc(nqb * 512)
    qhb = [ubc(512) for _ in range(2)]
    qhr = [ubc(qcW) for _ in range(2)]
    expand = ubc(NKV)
    cmask = ubc(nqb * ncc)
    cmaskT = ubc(nqb * 128)
    swm4 = ubc(4 * 128)
    dgsel = ubc(2 * 128)
    mrow = ub[0:1, uo:uo + qcW]
    uo += qcW
    negr4 = ub[0:1, uo:uo + 512]
    uo += 512
    selc = ubc(max(niw, 1))
    cntT = ubc(128)
    assert uo <= ubW
    csb = m.sb(f"{pr}csb", [128, 512])
    impb = m.sb(f"{pr}impb", [128, max(niw, 1)])
    m8 = m.sb(f"{pr}m8", [128, 16])
    mcol = m.sb(f"{pr}mcol", [128, 8])
    ETb = m.sb(f"{pr}ETb", [128, 512], BF16)
    cvev = m.sb(f"{pr}cvevb", [64, 128], BF16)
    onesr = m.sb(f"{pr}onesr", [1, 128])
    onesrb = m.sb(f"{pr}onesrb", [1, 128], BF16)
    onesc = m.sb(f"{pr}onesc", [128, 1], BF16)
    lnpr = f"{pr}LN"
    growW = max(4 * 128, 2 * qcW)
    lnrows = m.sb(f"{lnpr}rows", [1, max(4 * ntok, growW + 512)])
    lnab = m.sb(f"{lnpr}ab", [128, max(2 * ntok, 3 * 512)])
    lnb = {"fresh": False,
           "xf": [m.sb(f"{lnpr}xf{i}", [128, ntok]) for i in range(2)],
           "x2": m.sb(f"{lnpr}x2", [128, ntok], BF16),
           "onc": m.sb(f"{lnpr}onc", [128, 1], BF16),
           "onr": m.sb(f"{lnpr}onr", [1, 128], F32),
           "rows": lnrows, "ab": lnab, "eps": m.sb(f"{lnpr}eps", [1, 1])}
    grow = lnrows[0:1, 0:growW]
    rrow = lnrows[0:1, growW:growW + 512]
    acc = lnab[:, 0:512]
    bcs = lnab[:, 512:1024]
    Ebuf = lnab[:, 1024:1536]
    xo = [lnab[:, i * ntok:(i + 1) * ntok] for i in range(2)]
    p.op("gpsimd", lambda g: g.memset(lnb["onc"][:, :], 1.0),
         writes=[lnpr + "onc"])
    p.op("gpsimd", lambda g: g.memset(lnb["onr"][:, :], 1.0),
         writes=[lnpr + "onr"])
    p.op("gpsimd", lambda g: g.memset(lnb["eps"][:, :], 1e-5),
         writes=[lnpr + "eps"])

    p.op("gpsimd", lambda g: g.memset(onesr[:, :], 1.0), writes=[pr + "onesr"])
    p.op("gpsimd", lambda g: g.memset(onesrb[:, :], 1.0), writes=[pr + "onesrb"])
    p.op("gpsimd", lambda g: g.memset(onesc[:, :], 1.0), writes=[pr + "onesc"])
    p.op("gpsimd", lambda g: g.memset(bufA[:, :], 0.0), writes=[f"{pr}SHa"])
    p.op("gpsimd", lambda g: g.memset(bufB[:, :], 0.0), writes=[f"{pr}SHb"])
    p.dma("qpool", identb[:, :], io["identb"], writes=[pr + "idb"])
    p.dma("qpool", Rb[:, :], io["l1a_R"], writes=["Rbconst"])
    p.dma_group("qpool", [
        (cosT[:, :], io["l1b_cos"], [], ["ropeconst"]),
        (sinT[:, :], io["l1b_sin"], [], ["ropeconst"]),
        (qcosT[:, :], io["l1b_qcos"], [], ["ropeconst"]),
        (qsinT[:, :], io["l1b_qsin"], [], ["ropeconst"]),
        (kpc[:, :], io["l1b_kpc"], [], [pr + "kpc"]),
        (bcols[:, :], io["l1b_bcols"], [], [pr + "kpc"]),
        (gbias[:, :], io["l1b_gbias"], [], [pr + "kpc"]),
        (memc[:, :], io["l1b_memc"], [], [pr + "kpc"]),
    ], sem_key=pr + "consts")

    _, R, rk, ubkeys, lnb = ln_prescale(p, m, io, ntok, f"{pr}LN", pool,
                                        ub=ub, bufs=lnb)
    for o in range(KT):
        wkey = f"{pr}w"
        p.dma("qpool", wbuf[:, :],
              io["l1b_wqkvT"][o].rearrange("p k f -> p (k f)"),
              writes=[wkey], sem_key=wkey)
        for c, cs in enumerate(chunks(ntok)):
            w = cs.stop - cs.start
            px, pxk = pool[f"P{c % 2}"], f"P{c % 2}"
            mms = [lambda t, kt=kt, cs=cs, w=w, px=px: t.matmul(
                px[:, 0:w], wbuf[:, kt * 128:(kt + 1) * 128],
                ub[:, kt * ntok:(kt + 1) * ntok][:, cs],
                start=(kt == 0), stop=(kt == KT - 1)) for kt in range(KT)]
            p.mm_group(mms, reads=[wkey] + ubkeys, writes=[pxk])
            p.op("vector", lambda v, cs=cs, w=w, px=px: v.tensor_copy(
                scr["qb"][:, 0:w], px[:, 0:w]), reads=[pxk], writes=[f"{pr}LNqb"])
            p.dma("qscalar", qrawd[o * 128:(o + 1) * 128, cs], scr["qb"][:, 0:w],
                  reads=[f"{pr}LNqb"], writes=[f"{pr}qrawd"], sem_key=pr + "qrawst")
            rope_tile(p, m, pool, pr, px, pxk, qcosT, qsinT, cs, Rb,
                      scr["qf"][:, 0:w], f"{pr}LNqf",
                      {"qf": scr["qf"][:, 0:w], "qb": scr["qb"][:, 0:w],
                       "t1": scr["t1"][:, 0:w]}, f"{pr}LN")
            p.op("vector", lambda v, w=w: v.tensor_copy(scr["qb"][:, 0:w],
                 scr["qf"][:, 0:w]), reads=[f"{pr}LNqf"], writes=[f"{pr}LNqb"])
            p.dma("qscalar", qroped[o * 128:(o + 1) * 128, cs], scr["qb"][:, 0:w],
                  reads=[f"{pr}LNqb"], writes=[f"{pr}qroped"], sem_key=pr + "qrost")
    gw = f"{pr}w"
    p.dma("qpool", wbuf[:, 0:KT * 96],
          io["l1b_wgT"][0].rearrange("p k f -> p (k f)"), writes=[gw], sem_key=gw)
    for c, cs in enumerate(chunks(ntok)):
        w = cs.stop - cs.start
        px, pxk = pool[f"P{c % 2}"], f"P{c % 2}"
        mms = [lambda t, kt=kt, cs=cs, w=w, px=px: t.matmul(
            px[0:96, 0:w], wbuf[:, kt * 96:(kt + 1) * 96],
            ub[:, kt * ntok:(kt + 1) * ntok][:, cs],
            start=(kt == 0), stop=(kt == KT - 1)) for kt in range(KT)]
        p.mm_group(mms, reads=[gw] + ubkeys, writes=[pxk])
        p.op("vector", lambda v, cs=cs, w=w, px=px: v.tensor_scalar_add(
            px[0:96, 0:w], px[0:96, 0:w], gbias[:, 0:1]),
            reads=[pxk, pr + "kpc"], writes=[pxk])
        p.op("scalar", lambda s, cs=cs, w=w, px=px: s.activation(
            csb[0:96, 0:w], px[0:96, 0:w], AF.Sigmoid),
            reads=[pxk], writes=[f"{pr}csb"])
        p.dma("qscalar", gated[:, cs], csb[0:96, 0:w], reads=[f"{pr}csb"],
              writes=[f"{pr}gated"], sem_key=pr + "gatest")

    for kv2 in range(2):
        _, R, rk, ubkeys, lnb = ln_prescale(
            p, m, io, ntok, f"{pr}LN", pool,
            xin=xkv_half(io, kv2, ntok), ub=ub, bufs=lnb)
        for vv in range(2):
            cin, cink = bufA, f"{pr}SHa"
            for o in range(8):
                wkey = f"{pr}w"
                p.dma("qpool", wbuf[:, :],
                      io["l1b_wqkvT"][32 + vv * 8 + o].rearrange("p k f -> p (k f)"),
                      writes=[wkey], sem_key=wkey)
                for c, cs in enumerate(chunks(ntok)):
                    w = cs.stop - cs.start
                    px, pxk = pool[f"P{c % 2}"], f"P{c % 2}"
                    mms = [lambda t, kt=kt, cs=cs, w=w, px=px: t.matmul(
                        px[:, 0:w], wbuf[:, kt * 128:(kt + 1) * 128],
                        ub[:, kt * ntok:(kt + 1) * ntok][:, cs],
                        start=(kt == 0), stop=(kt == KT - 1)) for kt in range(KT)]
                    p.mm_group(mms, reads=[wkey] + ubkeys, writes=[pxk])
                    nblk_c = w // 16
                    blk0 = cs.start // 16
                    dst = AP(cin, o * 64 + blk0,
                             [[16 * 512, 128], [512, 16], [1, nblk_c]])
                    src3 = px[:, 0:w].rearrange("p (b q) -> p q b", q=16)
                    kpb = kpc[:, vv * 128 + o * 16:vv * 128 + (o + 1) * 16]
                    p.op("vector", lambda v, dst=dst, src3=src3, kpb=kpb,
                         nblk_c=nblk_c: v.tensor_tensor(
                        dst, src3, kpb.unsqueeze(2).broadcast_to(
                            [128, 16, nblk_c]), ALU.add),
                        reads=[pxk, pr + "kpc"], writes=[cink])
                    if vv == 0:
                        gcols = slice(kv2 * ntok + cs.start, kv2 * ntok + cs.stop)
                        rope_tile(p, m, pool, pr, px, pxk, cosT, sinT, gcols, Rb,
                                  scr["qb"][:, 0:w], f"{pr}LNqb",
                                  {"qf": scr["qf"][:, 0:w], "qb": scr["qb"][:, 0:w],
                                   "t1": scr["t1"][:, 0:w]}, f"{pr}LN")
                        for c8 in range(w // 128):
                            gkb = (kv2 * ntok + cs.start) // 128 + c8
                            p.dma("qscalar", krd[o, gkb, :, :],
                                  scr["qb"][:, c8 * 128:(c8 + 1) * 128],
                                  reads=[f"{pr}LNqb"], writes=[f"{pr}krd"],
                                  sem_key=pr + "krdst")
                    else:
                        p.op("vector", lambda v, cs=cs, w=w, px=px: v.tensor_copy(
                            scr["qb"][:, 0:w], px[:, 0:w]),
                            reads=[pxk], writes=[f"{pr}LNqb"])
                        for c8 in range(w // 128):
                            gkb = (kv2 * ntok + cs.start) // 128 + c8
                            pv, pvk = (pool["PT"], "PT") if c8 % 2 == 0 else \
                                (pool["PT2"], "PT2")
                            p.op("tensor", lambda t, c8=c8, pv=pv: t.transpose(
                                pv[:, 0:128], scr["qb"][:, c8 * 128:(c8 + 1) * 128],
                                identb[:, :]),
                                reads=[f"{pr}LNqb", pr + "idb"], writes=[pvk])
                            p.op("vector", lambda v, pv=pv: v.tensor_copy(
                                ETb[:, 0:128], pv[:, 0:128]),
                                reads=[pvk], writes=[f"{pr}ETb"])
                            p.dma("qscalar", vtd[o, gkb, :, :], ETb[:, 0:128],
                                  reads=[f"{pr}ETb"], writes=[f"{pr}vtd"],
                                  sem_key=pr + "vtdst")
            bl0 = kv2 * nbl
            hid, hidk = bufB, f"{pr}SHb"
            wT1 = io["l1b_kw1T"] if vv == 0 else io["l1b_vw1T"]
            wT2 = io["l1b_kw2T"] if vv == 0 else io["l1b_vw2T"]
            for o in range(16):
                wkey = f"{pr}w"
                p.dma("qpool", wbuf[:, 0:16 * 128],
                      wT1[o].rearrange("p k f -> p (k f)"),
                      writes=[wkey], sem_key=wkey)
                px, pxk = pool["P2"], "P2"
                mms = [lambda t, kt=kt, px=px: t.matmul(
                    px[:, 0:512], wbuf[:, kt * 128:(kt + 1) * 128],
                    cin[:, kt * 512:(kt + 1) * 512],
                    start=(kt == 0), stop=(kt == 15)) for kt in range(16)]
                p.mm_group(mms, reads=[wkey, cink], writes=[pxk])
                bc = bcols[:, (vv * 2) * 16 + o:(vv * 2) * 16 + o + 1]
                p.op("vector", lambda v, px=px, bc=bc: v.tensor_scalar_add(
                    px[:, 0:512], px[:, 0:512], bc),
                    reads=[pxk, pr + "kpc"], writes=[pxk])
                p.op("scalar", lambda s, o=o, px=px, hid=hid: s.activation(
                    hid[:, o * 512:(o + 1) * 512], px[:, 0:512], AF.Relu),
                    reads=[pxk], writes=[hidk])
            wkey = f"{pr}w"
            p.dma("qpool", wbuf[:, 0:16 * 128],
                  wT2[0].rearrange("p k f -> p (k f)"), writes=[wkey],
                  sem_key=wkey)
            px, pxk = pool["P2"], "P2"
            mms = [lambda t, kt=kt, px=px, hid=hid: t.matmul(
                px[:, 0:512], wbuf[:, kt * 128:(kt + 1) * 128],
                hid[:, kt * 512:(kt + 1) * 512],
                start=(kt == 0), stop=(kt == 15)) for kt in range(16)]
            p.mm_group(mms, reads=[wkey, hidk], writes=[pxk])
            bc = bcols[:, (vv * 2 + 1) * 16:(vv * 2 + 1) * 16 + 1]
            p.op("vector", lambda v, px=px, bc=bc: v.tensor_scalar_add(
                px[:, 0:512], px[:, 0:512], bc), reads=[pxk, pr + "kpc"],
                writes=[pxk])
            nb_eff = min(nbl, ncc - 1 - bl0)
            if vv == 0:
                for h in range(8):
                    if nb_eff <= 0:
                        continue
                    p.op("vector", lambda v, h=h, px=px, nb_eff=nb_eff, bl0=bl0:
                         v.tensor_copy(
                        ckT[:, h * ncc + 1 + bl0:h * ncc + 1 + bl0 + nb_eff],
                        px[:, h * 64:h * 64 + nb_eff]),
                        reads=[pxk], writes=[f"{pr}ckT"])
            else:
                p.op("vector", lambda v, px=px: v.tensor_copy(
                    ETb[:, 0:512], px[:, 0:512]), reads=[pxk], writes=[f"{pr}ETb"])
                for h in range(8):
                    if nb_eff <= 0:
                        continue
                    pv, pvk = (pool["PT"], "PT") if h % 2 == 0 else \
                        (pool["PT2"], "PT2")
                    p.op("tensor", lambda t, h=h, pv=pv: t.transpose(
                        pv[0:64, 0:128], ETb[:, h * 64:h * 64 + 64],
                        identb[:, :]),
                        reads=[f"{pr}ETb", pr + "idb"], writes=[pvk])
                    p.op("vector", lambda v, pv=pv: v.tensor_copy(
                        cvev[0:64, :], pv[0:64, 0:128]),
                        reads=[pvk], writes=[f"{pr}cvev"])
                    p.dma("qscalar", cvd[h, 1 + bl0:1 + bl0 + nb_eff, :],
                          cvev[0:nb_eff, :], reads=[f"{pr}cvev"],
                          writes=[f"{pr}cvd"], sem_key=pr + "cvdst")
        if kv2 == 0:
            for h in range(8):
                p.dma("qpool", AP(cvd.tensor, (h * 128 + 0) * 128,
                                  [[1, 128], [1, 1]]),
                      memc[:, 8 + h:8 + h + 1], reads=[pr + "kpc"],
                      writes=[f"{pr}cvd"], sem_key=pr + "cvdst2")
    for h in range(8):
        p.op("vector", lambda v, h=h: v.tensor_copy(
            ckT[:, h * ncc:h * ncc + 1], memc[:, h:h + 1]),
            reads=[pr + "kpc"], writes=[f"{pr}ckT"])

    p.barrier()
    p.dma_group("qpool", [
        (expand[:, :], io["l1b_expand"], [], [pr + "kpc"]),
        (cmask[:, :], io["l1b_cmask"], [], [pr + "kpc"]),
        (cmaskT[:, :], io["l1b_cmaskT"], [], [pr + "kpc"]),
        (dgsel[:, :], io["l1b_dgsel"], [], [pr + "kpc"]),
        (swm4[:, :], io["l1b_swm4"], [], [pr + "kpc"]),
    ], sem_key=pr + "consts2")
    p.op("gpsimd", lambda g: g.memset(cntT[:, :], 0.0), writes=[pr + "cntTz"])
    for h in range(8):
        p.dma("qsync", cvtm[0:ncc, h * 128:(h + 1) * 128], cvd[h, 0:ncc, :],
              reads=[f"{pr}cvd"], writes=[f"{pr}cvtm"], sem_key=pr + "cvld")

    gql = lambda qb: nqb + qb
    for kvh in range(8):
        p.dma_group("qsync", [
            (krb[:, kb * 128:(kb + 1) * 128], krd[kvh, kb, :, :],
             [f"{pr}krd"], [f"{pr}krb"]) for kb in range(nkb)],
            sem_key=pr + "krbld")
        p.dma_group("qsync", [
            (vtb[:, kb * 128:(kb + 1) * 128], vtd[kvh, kb, :, :],
             [f"{pr}vtd"], [f"{pr}vtb"]) for kb in range(nkb)],
            sem_key=pr + "vtbld")
        for qc2m in range(nqc):
            for qb in range(qc2m * nj4, (qc2m + 1) * nj4):
                ps_tm, ps_tmk = pool["P0"], "P0"
                qcols = slice(qb * 128, (qb + 1) * 128)
                p.dma_group("qsync", [
                    (qhb[0][:, g * 128:(g + 1) * 128],
                     qrawd[(4 * kvh + g) * 128:(4 * kvh + g + 1) * 128, qcols],
                     [f"{pr}qrawd"], [f"{pr}qhb0"]) for g in range(4)],
                    sem_key=pr + "qbld")
                mms = [lambda t, g=g, kvh=kvh: t.matmul(
                    ps_tm[:, g * ncc:(g + 1) * ncc],
                    qhb[0][:, g * 128:(g + 1) * 128],
                    ckT[:, kvh * ncc:(kvh + 1) * ncc],
                    start=True, stop=True) for g in range(4)]
                p.mm_group(mms, reads=[f"{pr}qhb0", f"{pr}ckT"], writes=[ps_tmk])
                cm = cmask[:, qb * ncc:(qb + 1) * ncc]
                p.op("vector", lambda v, cm=cm: v.tensor_tensor(
                    ps_tm[:, 0:4 * ncc].rearrange("p (g c) -> p g c", g=4),
                    ps_tm[:, 0:4 * ncc].rearrange("p (g c) -> p g c", g=4),
                    cm.unsqueeze(1).broadcast_to([128, 4, ncc]), ALU.add),
                    reads=[ps_tmk, pr + "kpc"], writes=[ps_tmk])
                p.op("vector", lambda v: v.tensor_copy(csb[:, 0:4 * ncc],
                     ps_tm[:, 0:4 * ncc]), reads=[ps_tmk], writes=[f"{pr}csb"])
                p.op("vector", lambda v: v.tensor_add(impb[:, 0:niw],
                     csb[:, 1:1 + niw], csb[:, ncc + 1:ncc + 1 + niw]),
                     reads=[f"{pr}csb"], writes=[f"{pr}impb"])
                p.op("vector", lambda v: v.tensor_add(Ebuf[:, 0:niw],
                     csb[:, 2 * ncc + 1:2 * ncc + 1 + niw],
                     csb[:, 3 * ncc + 1:3 * ncc + 1 + niw]),
                     reads=[f"{pr}csb"], writes=[f"{pr}Ebuf"])
                p.op("vector", lambda v: v.tensor_add(impb[:, 0:niw],
                     impb[:, 0:niw], Ebuf[:, 0:niw]),
                     reads=[f"{pr}impb", f"{pr}Ebuf"], writes=[f"{pr}impb"])
                p.op("scalar", lambda s: s.mul(impb[:, 0:niw], impb[:, 0:niw], 0.25),
                     reads=[f"{pr}impb"], writes=[f"{pr}impb"])
                p.op("vector", lambda v: v.reduce_max(
                    mcol[:, 0:4].rearrange("p (g one) -> p g one", one=1),
                    ps_tm[:, 0:4 * ncc].rearrange("p (g c) -> p g c", g=4),
                    axis=AX.X, negate=True), reads=[ps_tmk], writes=[f"{pr}mc"])
                for g in range(4):
                    p.op("vector", lambda v, g=g: v.tensor_copy(
                        scr["qb"][:, 0:1], mcol[:, g:g + 1]),
                        reads=[f"{pr}mc"], writes=[f"{pr}LNqb"])
                    p.op("tensor", lambda t: t.transpose(pool["PT"][0:1, 0:128],
                         scr["qb"][:, 0:1], identb[:, :]),
                         reads=[f"{pr}LNqb", pr + "idb"], writes=["PT"])
                    p.op("vector", lambda v, g=g: v.tensor_copy(
                        negr4[0:1, g * 128:(g + 1) * 128], pool["PT"][0:1, 0:128]),
                        reads=["PT"], writes=[f"{pr}negr"])
                ps_c, ps_ck = pool["P1"], "P1"
                p.mm_group([
                    lambda t, kvh=kvh: t.matmul(
                        ps_c[0:ncc, 0:512],
                        ckT[:, kvh * ncc:(kvh + 1) * ncc],
                        qhb[0][:, 0:512],
                        start=True, stop=False),
                    lambda t: t.matmul(
                        ps_c[0:ncc, 0:512], onesrb[0:1, 0:ncc],
                        negr4[0:1, 0:512],
                        start=False, stop=True)],
                    reads=[f"{pr}qhb0", f"{pr}ckT", f"{pr}negr", pr + "onesrb"],
                    writes=[ps_ck])
                cmT = cmaskT[:, qb * 128:(qb + 1) * 128]
                p.op("vector", lambda v, cmT=cmT: v.tensor_tensor(
                    ps_c[0:ncc, 0:512].rearrange("p (g q) -> p g q", g=4),
                    ps_c[0:ncc, 0:512].rearrange("p (g q) -> p g q", g=4),
                    cmT[0:ncc, :].unsqueeze(1).broadcast_to([ncc, 4, 128]), ALU.add),
                    reads=[ps_ck, pr + "kpc"], writes=[ps_ck])
                p.op("scalar", lambda s: s.activation(ETb[0:ncc, 0:512],
                     ps_c[0:ncc, 0:512], AF.Exp, scale=scale),
                     reads=[ps_ck], writes=[f"{pr}ETb"])
                ps_co, ps_cok = pool["P2"], "P2"
                ps_dn, ps_dnk = pool["P3"], "P3"
                p.mm_group([lambda t, kvh=kvh: t.matmul(
                    ps_co[:, 0:512],
                    cvtm[0:ncc, kvh * 128:(kvh + 1) * 128],
                    ETb[0:ncc, 0:512],
                    start=True, stop=True)],
                    reads=[f"{pr}ETb", f"{pr}cvtm"], writes=[ps_cok])
                p.mm_group([lambda t: t.matmul(
                    ps_dn[0:1, 0:512], onesc[0:ncc, :],
                    ETb[0:ncc, 0:512],
                    start=True, stop=True)],
                    reads=[f"{pr}ETb", pr + "onesc"], writes=[ps_dnk])
                p.dma("qsync", grow[0:1, 0:4 * 128],
                      AP(gated.tensor, 12 * kvh * ntok + qb * 128,
                         [[12 * ntok, 1], [3 * ntok, 4], [1, 128]]),
                      reads=[f"{pr}gated"], writes=[f"{pr}grow"],
                      sem_key=pr + "grld")
                p.op("vector", lambda v: v.tensor_copy(rrow[0:1, 0:512],
                     ps_dn[0:1, 0:512]), reads=[ps_dnk], writes=[f"{pr}rrow"])
                p.op("vector", lambda v: v.reciprocal(rrow[0:1, 0:512],
                     rrow[0:1, 0:512]), reads=[f"{pr}rrow"], writes=[f"{pr}rrow"])
                p.op("vector", lambda v: v.tensor_mul(rrow[0:1, 0:512],
                     rrow[0:1, 0:512], grow[0:1, 0:512]),
                     reads=[f"{pr}rrow", f"{pr}grow"], writes=[f"{pr}rrow"])
                p.mm_group([lambda t: t.matmul(pool["P4"][:, 0:512], onesr[:, :],
                                               rrow[0:1, 0:512], start=True,
                                               stop=True)],
                           reads=[f"{pr}rrow", pr + "onesr"], writes=["P4"])
                p.op("vector", lambda v: v.tensor_copy(bcs[:, 0:512],
                     pool["P4"][:, 0:512]), reads=["P4"], writes=[f"{pr}bcs"])
                p.op("vector", lambda v, qb=qb: v.tensor_mul(
                    cob[:, qb * 512:(qb + 1) * 512], ps_co[:, 0:512], bcs[:, 0:512]),
                    reads=[ps_cok, f"{pr}bcs"], writes=[f"{pr}cob:{qb}"])
                if globals().get("_L1AB_DEBUG") and kvh == 0 and qb == 0:
                    dbgp = nc.dram_tensor(pr + "dbgp", [4, 128, 512], F32).ap()
                    p.op("vector", lambda v: v.tensor_copy(acc[:, 0:512],
                         ETb[:, 0:512]), reads=[f"{pr}ETb"], writes=[f"{pr}acc"])
                    p.dma("qscalar", dbgp[0], acc[:, 0:512], reads=[f"{pr}acc"],
                          writes=[f"{pr}dbgP"], sem_key=pr + "dbgpst")
                    p.op("vector", lambda v: v.tensor_copy(acc[:, 0:512],
                         ps_co[:, 0:512]), reads=[ps_cok], writes=[f"{pr}acc"])
                    p.dma("qscalar", dbgp[1], acc[:, 0:512], reads=[f"{pr}acc"],
                          writes=[f"{pr}dbgP"], sem_key=pr + "dbgpst")
                    p.op("vector", lambda v: v.tensor_copy(acc[0:1, 0:512],
                         ps_dn[0:1, 0:512]), reads=[ps_dnk], writes=[f"{pr}acc"])
                    p.dma("qscalar", dbgp[2, 0:1, :], acc[0:1, 0:512],
                          reads=[f"{pr}acc"], writes=[f"{pr}dbgP"],
                          sem_key=pr + "dbgpst")
                    p.op("vector", lambda v: v.tensor_copy(acc[:, 0:512],
                         bcs[:, 0:512]), reads=[f"{pr}bcs"], writes=[f"{pr}acc"])
                    p.dma("qscalar", dbgp[3], acc[:, 0:512], reads=[f"{pr}acc"],
                          writes=[f"{pr}dbgP"], sem_key=pr + "dbgpst")
                p.op("vector", lambda v: v.reduce_max(mcol[:, 4:5], impb[:, 0:niw],
                     axis=AX.X), reads=[f"{pr}impb"], writes=[f"{pr}selM"])
                p.op("vector", lambda v: v.tensor_scalar_max(mcol[:, 4:5],
                     mcol[:, 4:5], -1000.0), reads=[f"{pr}selM"], writes=[f"{pr}selM"])
                p.op("vector", lambda v: v.max(out=m8[:, 0:8], in_=impb[:, 0:niw]),
                     reads=[f"{pr}impb"], writes=[f"{pr}m8"])
                p.op("vector", lambda v: v.tensor_scalar_mul(mcol[:, 5:6],
                     mcol[:, 4:5], -1.0), reads=[f"{pr}selM"], writes=[f"{pr}negM"])
                p.op("scalar", lambda s: s.activation(Ebuf[:, 0:niw], impb[:, 0:niw],
                     AF.Exp, bias=mcol[:, 5:6], accum_out=mcol[:, 6:7]),
                     reads=[f"{pr}impb", f"{pr}negM"], writes=[f"{pr}Ebuf", f"{pr}dsum"])
                p.op("vector", lambda v: v.tensor_scalar_add(mcol[:, 7:8],
                     mcol[:, 4:5], 1000.0), reads=[f"{pr}selM"], writes=[f"{pr}t8"])
                p.op("scalar", lambda s: s.activation(mcol[:, 7:8], mcol[:, 7:8],
                     AF.Exp, scale=-1.0), reads=[f"{pr}t8"], writes=[f"{pr}t8"])
                p.op("vector", lambda v: v.tensor_add(mcol[:, 6:7], mcol[:, 6:7],
                     mcol[:, 7:8]), reads=[f"{pr}dsum", f"{pr}t8"],
                     writes=[f"{pr}dsum"])
                p.op("scalar", lambda s: s.activation(mcol[:, 6:7], mcol[:, 6:7],
                     AF.Ln), reads=[f"{pr}dsum"], writes=[f"{pr}dsum"])
                p.op("vector", lambda v: v.tensor_add(mcol[:, 6:7], mcol[:, 6:7],
                     mcol[:, 4:5]), reads=[f"{pr}dsum", f"{pr}selM"],
                     writes=[f"{pr}dsum"])
                p.op("vector", lambda v: v.tensor_scalar_add(mcol[:, 6:7],
                     mcol[:, 6:7], float(np.log(1e-10))),
                     reads=[f"{pr}dsum"], writes=[f"{pr}dsum"])
                p.op("vector", lambda v: v.tensor_scalar(Ebuf[:, 0:niw],
                     impb[:, 0:niw], m8[:, 3:4], None, ALU.is_ge),
                     reads=[f"{pr}impb", f"{pr}m8"], writes=[f"{pr}Ebuf"])
                p.op("vector", lambda v: v.tensor_scalar(selc[:, 0:niw],
                     impb[:, 0:niw], mcol[:, 6:7], None, ALU.is_gt),
                     reads=[f"{pr}impb", f"{pr}dsum"], writes=[f"{pr}selc"])
                p.op("vector", lambda v: v.tensor_mul(selc[:, 0:niw], selc[:, 0:niw],
                     Ebuf[:, 0:niw]), reads=[f"{pr}selc", f"{pr}Ebuf"],
                     writes=[f"{pr}selc"])
                p.op("tensor", lambda t: t.transpose(pool["PT2"][0:niw, 0:128],
                     selc[:, 0:niw], identb[:, :]),
                     reads=[f"{pr}selc", pr + "idb"], writes=["PT2"])
                p.op("vector", lambda v: v.tensor_copy(cntT[0:niw, :],
                     pool["PT2"][0:niw, 0:128]), reads=["PT2", pr + "cntTz"],
                     writes=[f"{pr}cntT:{qb}"])
                for kb in range(gql(qb) + 1):
                    px5, px5k = pool["P5"], "P5"
                    p.mm_group([lambda t, kb=kb: t.matmul(
                        px5[:, 0:128], expand[:, kb * 128:(kb + 1) * 128],
                        cntT[0:128, :], start=True, stop=True)],
                        reads=[f"{pr}cntT:{qb}", pr + "cntTz", pr + "kpc"],
                        writes=[px5k])
                    if kb == qb:
                        p.op("vector", lambda v, px5=px5: v.tensor_add(
                            px5[:, 0:128], px5[:, 0:128], dgsel[:, 0:128]),
                            reads=[px5k, pr + "kpc"], writes=[px5k])
                    if kb == gql(qb):
                        p.op("vector", lambda v, px5=px5: v.tensor_add(
                            px5[:, 0:128], px5[:, 0:128], dgsel[:, 128:256]),
                            reads=[px5k, pr + "kpc"], writes=[px5k])
                    qc2 = qb // nj4
                    j4 = qb % nj4
                    p.op("vector", lambda v, kb=kb, qc2=qc2, j4=j4, px5=px5:
                         v.tensor_copy(
                        cntjT[:, kb * qcW + j4 * 128:kb * qcW + (j4 + 1) * 128],
                        px5[:, 0:128]), reads=[px5k],
                        writes=[f"{pr}cntjT:{kb}:{j4}"])
                qc2 = qb // nj4
                j4 = qb % nj4
                qb0z = nqb + qc2 * nj4
                for kb in range(gql(qb) + 1, qb0z + nj4):
                    p.op("gpsimd", lambda g, kb=kb, j4=j4: g.memset(
                        cntjT[:, kb * qcW + j4 * 128:kb * qcW + (j4 + 1) * 128],
                        0.0), writes=[f"{pr}cntjT:{kb}:{j4}"])
            for g in range(4):
                H = 4 * kvh + g
                for qc2 in [qc2m]:
                    qbuf, qbk = qhr[qc2 % 2], f"{pr}qhr{qc2 % 2}"
                    p.dma("qsync", qbuf[:, 0:qcW],
                          qroped[H * 128:(H + 1) * 128, qc2 * qcW:(qc2 + 1) * qcW],
                          reads=[f"{pr}qroped"], writes=[qbk], sem_key=qbk)
                    qb0 = nqb + qc2 * nj4
                    gq_max = qb0 + nj4 - 1
                    for j4 in range(nj4):
                        nkq = (qb0 + j4 + 1) * 128
                        for kc, kcs in enumerate(chunks(nkq)):
                            w = kcs.stop - kcs.start
                            px, pxk = pool["P0"], "P0"
                            kb0 = kcs.start // 128
                            nkbc = w // 128
                            mms = []
                            for i in range(nkbc):
                                kb = kb0 + i
                                mms.append(lambda t, j4=j4, kb=kb, i=i, px=px,
                                           qbuf=qbuf: t.matmul(
                                    px[:, i * 128:(i + 1) * 128],
                                    qbuf[:, j4 * 128:(j4 + 1) * 128],
                                    krb[:, kb * 128:(kb + 1) * 128],
                                    start=True, stop=True))
                            p.mm_group(mms, reads=[qbk, f"{pr}krb"], writes=[pxk])
                            dst = mcol[:, 0:1] if kc == 0 else mcol[:, 1:2]
                            dk = f"{pr}fm{0 if kc == 0 else 1}"
                            p.op("vector", lambda v, px=px, w=w, dst=dst:
                                 v.reduce_max(dst, px[:, 0:w], axis=AX.X),
                                 reads=[pxk], writes=[dk])
                            if kc > 0:
                                p.op("vector", lambda v: v.tensor_max(
                                    mcol[:, 0:1], mcol[:, 0:1], mcol[:, 1:2]),
                                    reads=[f"{pr}fm0", f"{pr}fm1"],
                                    writes=[f"{pr}fm0"])
                        p.op("vector", lambda v: v.tensor_scalar_mul(
                            mcol[:, 1:2], mcol[:, 0:1], -1.0),
                            reads=[f"{pr}fm0"], writes=[f"{pr}fm1"])
                        p.op("vector", lambda v: v.tensor_copy(
                            scr["qb"][:, 0:1], mcol[:, 1:2]),
                            reads=[f"{pr}fm1"], writes=[f"{pr}LNqb"])
                        p.op("tensor", lambda t: t.transpose(
                            pool["PT"][0:1, 0:128], scr["qb"][:, 0:1], identb[:, :]),
                            reads=[f"{pr}LNqb", pr + "idb"], writes=["PT"])
                        p.op("vector", lambda v, j4=j4: v.tensor_copy(
                            mrow[0:1, j4 * 128:(j4 + 1) * 128],
                            pool["PT"][0:1, 0:128]), reads=["PT"],
                            writes=[f"{pr}mrowf"])
                    for kb in range(gq_max + 1):
                        ps_f, ps_fk = pool["P1"], "P1"
                        p.mm_group([
                            lambda t, kb=kb, qbuf=qbuf: t.matmul(
                                ps_f[:, 0:qcW], krb[:, kb * 128:(kb + 1) * 128],
                                qbuf[:, 0:qcW], start=True, stop=False),
                            lambda t: t.matmul(
                                ps_f[:, 0:qcW], onesrb[:, :], mrow[0:1, 0:qcW],
                                start=False, stop=True)],
                            reads=[qbk, f"{pr}krb", f"{pr}mrowf", pr + "onesrb"],
                            writes=[ps_fk])
                        p.op("scalar", lambda s, ps_f=ps_f: s.activation(
                            Ebuf[:, 0:qcW], ps_f[:, 0:qcW], AF.Exp, scale=scale),
                            reads=[ps_fk], writes=[f"{pr}Ebuf"])
                        cjkeys = [f"{pr}cntjT:{kb}:{j}" for j in range(nj4)]
                        p.op("vector", lambda v, kb=kb: v.tensor_mul(
                            ETb[:, 0:qcW], Ebuf[:, 0:qcW],
                            cntjT[:, kb * qcW:(kb + 1) * qcW]),
                            reads=[f"{pr}Ebuf"] + cjkeys, writes=[f"{pr}ETb"])
                        p.mm_group([lambda t, kb=kb, st=(kb == 0),
                                    sp=(kb == gq_max): t.matmul(
                            pool["P2"][:, 0:qcW], vtb[:, kb * 128:(kb + 1) * 128],
                            ETb[:, 0:qcW], start=st, stop=sp)],
                            reads=[f"{pr}ETb", f"{pr}vtb"], writes=["P2"])
                        p.mm_group([lambda t, st=(kb == 0), sp=(kb == gq_max):
                                    t.matmul(
                            pool["P3"][0:1, 0:qcW], onesc[:, :], ETb[:, 0:qcW],
                            start=st, stop=sp)],
                            reads=[f"{pr}ETb", pr + "onesc"], writes=["P3"])
                    for j4 in range(nj4):
                        gq = qb0 + j4
                        qsl = slice(j4 * 128, (j4 + 1) * 128)
                        first = True
                        cands = [gq - nqb - 1, gq - nqb, gq - 1, gq]
                        for kk2, kb in enumerate(cands):
                            if kb < 0:
                                continue
                            ps_s, ps_sk = pool["P1"], "P1"
                            p.mm_group([
                                lambda t, kb=kb, qsl=qsl, qbuf=qbuf: t.matmul(
                                    ps_s[:, 0:128], krb[:, kb * 128:(kb + 1) * 128],
                                    qbuf[:, qsl], start=True, stop=False),
                                lambda t, qsl=qsl: t.matmul(
                                    ps_s[:, 0:128], onesrb[:, :], mrow[0:1, qsl],
                                    start=False, stop=True)],
                                reads=[qbk, f"{pr}krb", f"{pr}mrowf", pr + "onesrb"],
                                writes=[ps_sk])
                            sw = swm4[:, kk2 * 128:(kk2 + 1) * 128]
                            p.op("vector", lambda v, sw=sw, ps_s=ps_s: v.tensor_add(
                                ps_s[:, 0:128], ps_s[:, 0:128], sw),
                                reads=[ps_sk, pr + "kpc"], writes=[ps_sk])
                            p.op("scalar", lambda s, ps_s=ps_s: s.activation(
                                ETb[:, 0:128], ps_s[:, 0:128], AF.Exp, scale=scale),
                                reads=[ps_sk], writes=[f"{pr}ETb"])
                            p.mm_group([lambda t, kb=kb, qsl=qsl, st=first,
                                        sp=(kb == gq): t.matmul(
                                pool["P4"][:, qsl], vtb[:, kb * 128:(kb + 1) * 128],
                                ETb[:, 0:128], start=st, stop=sp)],
                                reads=[f"{pr}ETb", f"{pr}vtb"], writes=["P4"])
                            p.mm_group([lambda t, qsl=qsl, st=first, sp=(kb == gq):
                                        t.matmul(
                                pool["P5"][0:1, qsl], onesc[:, :], ETb[:, 0:128],
                                start=st, stop=sp)],
                                reads=[f"{pr}ETb", pr + "onesc"], writes=["P5"])
                            first = False
                    p.dma("qsync", grow[0:1, 0:2 * qcW],
                          AP(gated.tensor, (12 * kvh + 3 * g + 1) * ntok + qc2 * qcW,
                             [[2 * ntok, 1], [ntok, 2], [1, qcW]]),
                          reads=[f"{pr}gated"], writes=[f"{pr}grow"],
                          sem_key=pr + "grld")
                    p.op("vector", lambda v: v.tensor_copy(rrow[0:1, 0:qcW],
                         pool["P3"][0:1, 0:qcW]), reads=["P3"], writes=[f"{pr}rrow"])
                    p.op("vector", lambda v: v.reciprocal(rrow[0:1, 0:qcW],
                         rrow[0:1, 0:qcW]), reads=[f"{pr}rrow"], writes=[f"{pr}rrow"])
                    p.op("vector", lambda v: v.tensor_mul(rrow[0:1, 0:qcW],
                         rrow[0:1, 0:qcW], grow[0:1, 0:qcW]),
                         reads=[f"{pr}rrow", f"{pr}grow"], writes=[f"{pr}rrow"])
                    p.mm_group([lambda t: t.matmul(pool["P0"][:, 0:qcW], onesr[:, :],
                                                   rrow[0:1, 0:qcW], start=True,
                                                   stop=True)],
                               reads=[f"{pr}rrow", pr + "onesr"], writes=["P0"])
                    p.op("vector", lambda v: v.tensor_copy(bcs[:, 0:qcW],
                         pool["P0"][:, 0:qcW]), reads=["P0"], writes=[f"{pr}bcs"])
                    p.op("vector", lambda v: v.tensor_mul(acc[:, 0:qcW],
                         pool["P2"][:, 0:qcW], bcs[:, 0:qcW]),
                         reads=["P2", f"{pr}bcs"], writes=[f"{pr}acc"])
                    p.op("vector", lambda v: v.tensor_copy(rrow[0:1, 0:qcW],
                         pool["P5"][0:1, 0:qcW]), reads=["P5"], writes=[f"{pr}rrow"])
                    p.op("vector", lambda v: v.reciprocal(rrow[0:1, 0:qcW],
                         rrow[0:1, 0:qcW]), reads=[f"{pr}rrow"], writes=[f"{pr}rrow"])
                    p.op("vector", lambda v: v.tensor_mul(rrow[0:1, 0:qcW],
                         rrow[0:1, 0:qcW], grow[0:1, qcW:2 * qcW]),
                         reads=[f"{pr}rrow", f"{pr}grow"], writes=[f"{pr}rrow"])
                    p.mm_group([lambda t: t.matmul(pool["P0"][:, 0:qcW], onesr[:, :],
                                                   rrow[0:1, 0:qcW], start=True,
                                                   stop=True)],
                               reads=[f"{pr}rrow", pr + "onesr"], writes=["P0"])
                    p.op("vector", lambda v: v.tensor_copy(bcs[:, 0:qcW],
                         pool["P0"][:, 0:qcW]), reads=["P0"], writes=[f"{pr}bcs"])
                    p.op("vector", lambda v: v.tensor_mul(Ebuf[:, 0:qcW],
                         pool["P4"][:, 0:qcW], bcs[:, 0:qcW]),
                         reads=["P4", f"{pr}bcs"], writes=[f"{pr}Ebuf"])
                    p.op("vector", lambda v: v.tensor_add(acc[:, 0:qcW],
                         acc[:, 0:qcW], Ebuf[:, 0:qcW]),
                         reads=[f"{pr}acc", f"{pr}Ebuf"], writes=[f"{pr}acc"])
                    cobks = [f"{pr}cob:{qc2 * nj4 + j}" for j in range(nj4)]
                    cobsl = AP(ub, cob_off + (qc2 * nj4) * 512 + g * 128,
                               [[ubW, 128], [512, nj4], [1, 128]])
                    p.op("vector", lambda v, cobsl=cobsl: v.tensor_add(
                         acc[:, 0:qcW].rearrange("p (a b) -> p a b", b=128), 
                         acc[:, 0:qcW].rearrange("p (a b) -> p a b", b=128), cobsl),
                         reads=[f"{pr}acc"] + cobks, writes=[f"{pr}acc"])
                    p.op("vector", lambda v: v.tensor_copy(ETb[:, 0:qcW],
                         acc[:, 0:qcW]), reads=[f"{pr}acc"], writes=[f"{pr}ETb"])
                    p.dma("qscalar", ctxd[H * 128:(H + 1) * 128,
                                          qc2 * qcW:(qc2 + 1) * qcW],
                          ETb[:, 0:qcW], reads=[f"{pr}ETb"], writes=[f"{pr}ctxd"],
                          sem_key=pr + "ctxst")

    p.barrier()
    tpp = min(16, 8192 // ntok)
    npass = KT // tpp
    for g2 in range(npass):
        cbuf, cbk = (bufA, f"{pr}SHa") if g2 % 2 == 0 else (bufB, f"{pr}SHb")
        p.dma_group("qsync", [
            (cbuf[:, i * ntok:(i + 1) * ntok],
             ctxd[(g2 * tpp + i) * 128:(g2 * tpp + i + 1) * 128, :],
             [f"{pr}ctxd"], [cbk]) for i in range(tpp)],
            sem_key=pr + f"ctxld{g2}")
        for o in range(KT):
            wkey = f"{pr}w"
            p.dma("qpool", wbuf[:, 0:tpp * 128],
                  io["l1b_woT"][o][:, g2 * tpp:(g2 + 1) * tpp, :].rearrange(
                      "p k f -> p (k f)"),
                  writes=[wkey], sem_key=wkey)
            xb, xbk = xo[o % 2], f"{pr}xo{o % 2}"
            if g2 == 0:
                p.dma("qsync", xb[:, :], io["xin"][128 * o:128 * (o + 1), :],
                      writes=[xbk], sem_key=xbk)
            else:
                p.dma("qsync", xb[:, :], io["xout"][128 * o:128 * (o + 1), :],
                      reads=[f"{pr}xod:{o}"], writes=[xbk], sem_key=xbk)
            for c, cs in enumerate(chunks(ntok)):
                w = cs.stop - cs.start
                px, pxk = pool[f"P{c % 2}"], f"P{c % 2}"
                mms = [lambda t, kt=kt, cs=cs, w=w, px=px, cbuf=cbuf: t.matmul(
                    px[:, 0:w], wbuf[:, kt * 128:(kt + 1) * 128],
                    cbuf[:, kt * ntok:(kt + 1) * ntok][:, cs],
                    start=(kt == 0), stop=(kt == tpp - 1)) for kt in range(tpp)]
                p.mm_group(mms, reads=[wkey, cbk], writes=[pxk])
                p.op("vector", lambda v, cs=cs, w=w, px=px, xb=xb:
                     v.scalar_tensor_tensor(xb[:, cs], px[:, 0:w], 1.0,
                                            xb[:, cs], ALU.mult, ALU.add),
                     reads=[pxk, xbk], writes=[xbk])
            p.dma("qscalar", io["xout"][128 * o:128 * (o + 1), :], xb[:, :],
                  reads=[xbk], writes=[f"{pr}xod:{o}"], sem_key=xbk + f"st{g2}")


def prep_l1ab(inp, half, ntok=1024):
    ios = {}
    NKV = 2 * ntok
    nqb = ntok // 128
    ncb = NKV // 16
    ncc = min(ncb, 128)
    niw = ncc - 1
    qoff = half * ntok
    perm = np.empty(128, np.int64)
    perm[np.arange(64) * 2] = np.arange(64)
    perm[np.arange(64) * 2 + 1] = 64 + np.arange(64)
    iperm = np.argsort(perm)

    wqkv = inp["L1_attn_b_w_qkv"].astype(np.float32).copy()
    for hh in range(32):
        wqkv[hh * 128:(hh + 1) * 128] = wqkv[hh * 128:(hh + 1) * 128][iperm]
    for hh in range(8):
        r0 = 4096 + hh * 128
        wqkv[r0:r0 + 128] = wqkv[r0:r0 + 128][iperm]
    ios["l1b_wqkvT"] = tile_lhsT(wqkv)
    ios["l1b_wgT"] = tile_lhsT(inp["L1_attn_b_w_gate"].astype(np.float32),
                               ofree=96)
    ios["l1b_woT"] = tile_lhsT(inp["L1_attn_b_w_o"].astype(np.float32))
    kw1 = inp["L1_attn_b_k_w1"].astype(np.float32).copy()
    kw1 = kw1.reshape(2048, 16, 128)[:, :, iperm].reshape(2048, 2048)
    ios["l1b_kw1T"] = tile_lhsT(kw1)
    ios["l1b_vw1T"] = tile_lhsT(inp["L1_attn_b_v_w1"].astype(np.float32))
    kw2 = inp["L1_attn_b_k_w2"].astype(np.float32)[iperm]
    ios["l1b_kw2T"] = tile_lhsT(kw2)
    ios["l1b_vw2T"] = tile_lhsT(inp["L1_attn_b_v_w2"].astype(np.float32))
    kpcols = np.zeros((128, 256), np.float32)
    kp = inp["L1_attn_b_k_pos"].astype(np.float32)[:, :, iperm]
    vp = inp["L1_attn_b_v_pos"].astype(np.float32)
    for hh in range(8):
        for ph in range(16):
            kpcols[:, hh * 16 + ph] = kp[hh, ph]
            kpcols[:, 128 + hh * 16 + ph] = vp[hh, ph]
    ios["l1b_kpc"] = kpcols
    bc = np.zeros((128, 64), np.float32)
    bc[:, 0:16] = inp["L1_attn_b_k_b1"].astype(np.float32).reshape(16, 128).T
    bc[:, 16] = inp["L1_attn_b_k_b2"].astype(np.float32)[iperm]
    bc[:, 32:48] = inp["L1_attn_b_v_b1"].astype(np.float32).reshape(16, 128).T
    bc[:, 48] = inp["L1_attn_b_v_b2"].astype(np.float32)
    ios["l1b_bcols"] = bc
    ios["l1b_gbias"] = inp["L1_attn_b_b_gate"].astype(np.float32).reshape(96, 1)
    memc = np.zeros((128, 16), np.float32)
    mk = inp["L1_attn_b_mem_kv"].astype(np.float32)
    for hh in range(8):
        memc[:, hh] = mk[0, hh, 0][iperm]
        memc[:, 8 + hh] = mk[1, hh, 0]
    ios["l1b_memc"] = memc
    ex = np.zeros((128, NKV), np.float32)
    for jp in range(NKV):
        jb = jp // 16
        if jb < niw:
            ex[jb, jp] = 1.0
    ios["l1b_expand"] = ex
    cm = np.zeros((128, nqb * ncc), np.float32)
    cmT = np.zeros((128, nqb * 128), np.float32)
    for qb in range(nqb):
        for r in range(128):
            i = qoff + qb * 128 + r
            for c in range(ncc):
                vis = (c == 0) or (16 * c - 1 < i)
                if not vis:
                    cm[r, qb * ncc + c] = -1e9
                    cmT[c, qb * 128 + r] = -1e9
    ios["l1b_cmask"] = cm
    ios["l1b_cmaskT"] = cmT
    dg = np.zeros((128, 128), np.float32)
    for kp_ in range(128):
        for r in range(128):
            if kp_ // 16 == r // 16 and kp_ % 16 <= r % 16:
                dg[kp_, r] = 1.0
    dgs = np.zeros((128, 256), np.float32)
    dgs[:, (0 if half == 0 else 128):(128 if half == 0 else 256)] = dg
    ios["l1b_dgsel"] = dgs
    sw = np.full((128, 256), -1e9, np.float32)
    for kp_ in range(128):
        for r in range(128):
            if kp_ >= r + 64:
                sw[kp_, r] = 0.0
            if 0 <= r - kp_ <= 64:
                sw[kp_, 128 + r] = 0.0
    sw4 = np.full((128, 512), -1e9, np.float32)
    if half == 0:
        sw4[:, 0:256] = sw
    else:
        sw4[:, 256:512] = sw
    ios["l1b_swm4"] = sw4
    S = NKV
    pos = np.arange(S, dtype=np.float64)
    inv = 1.0 / 10000.0 ** (np.arange(0, 128, 2, dtype=np.float64) / 128)
    ang = pos[:, None] * inv[None, :]
    cos = np.concatenate([np.cos(ang), np.cos(ang)], -1)
    sin = np.concatenate([np.sin(ang), np.sin(ang)], -1)
    ios["l1b_cos"] = np.ascontiguousarray(cos.T.astype(np.float32))
    ios["l1b_sin"] = np.ascontiguousarray(sin.T.astype(np.float32))
    ios["l1b_qcos"] = np.ascontiguousarray(ios["l1b_cos"][:, qoff:qoff + ntok])
    ios["l1b_qsin"] = np.ascontiguousarray(ios["l1b_sin"][:, qoff:qoff + ntok])
    Rm = np.zeros((128, 128), np.float32)
    for mm_ in range(64):
        Rm[mm_ + 64, mm_] = -1.0
        Rm[mm_, mm_ + 64] = 1.0
    ios["l1a_R"] = Rm
    return ios


STAGE_EMITTERS["l1ab_h0"] = lambda p, m, io, ntok, pool: emit_l1ab(
    p, m, io, ntok, pool, half=0)
STAGE_EMITTERS["l1ab_h1"] = lambda p, m, io, ntok, pool: emit_l1ab(
    p, m, io, ntok, pool, half=1)


def build_stage_program(stage, ntok, host_arrays, percore=None):
    nc = bass.Bass("TRN2", target_bir_lowering=False, debug=False)
    ctx = ExitStack()
    io = {}
    io["xin"] = nc.dram_tensor("xin", [D, ntok], F32, kind="ExternalInput").ap()
    io["xout"] = nc.dram_tensor("xout", [D, ntok], F32, kind="ExternalOutput").ap()
    for name, shape in (percore or {}).items():
        io[name] = nc.dram_tensor(name, list(shape), F32,
                                  kind="ExternalInput").ap()
    for name, arr in host_arrays.items():
        io[name] = nc.dram_tensor(name, list(arr.shape), F32,
                                  kind="ExternalInput").ap()
    m = Mem(nc, ctx)
    pool = make_pool(m)
    p = Prog(nc, ctx)
    STAGE_EMITTERS[stage](p, m, io, ntok, pool)
    p.finish("sync")
    p.emit()
    ctx.close()
    return nc


STAGE_EMITTERS["l0ffn"] = emit_l0ffn


def cplx_mul(p, m, pr, oR, oI, aR, aI, bR, bI, keys, okeys, conj_a=False,
             tmp=None, tmpk=None):
    s = 1.0 if conj_a else -1.0
    p.op("vector", lambda v: v.tensor_mul(oR, aR, bR), reads=keys, writes=[okeys[0]])
    p.op("vector", lambda v: v.tensor_mul(tmp, aI, bI), reads=keys, writes=[tmpk])
    p.op("vector", lambda v: v.scalar_tensor_tensor(oR, tmp, s, oR, ALU.mult,
         ALU.add), reads=[tmpk, okeys[0]], writes=[okeys[0]])
    p.op("vector", lambda v: v.tensor_mul(oI, aR, bI), reads=keys, writes=[okeys[1]])
    p.op("vector", lambda v: v.tensor_mul(tmp, aI, bR), reads=keys, writes=[tmpk])
    p.op("vector", lambda v: v.scalar_tensor_tensor(oI, tmp, -s, oI, ALU.mult,
         ALU.add), reads=[tmpk, okeys[1]], writes=[okeys[1]])


def emit_l0ab(p, m, io, ntok, pool, pr="S", truncate=None):
    nc = m.nc
    cs_l = chunks(ntok)
    ub, R, rk, ubkeys, _lb = ln_prescale(p, m, io, ntok, pr, pool)

    cells = m.sb(f"{pr}cells", [1, 8])
    ck = lambda i: f"{pr}cell{i}"
    p.dma("qpool", cells[0:1, 0:1], io["l0b_log_step"], writes=[ck(0)])
    p.op("scalar", lambda s: s.activation(cells[0:1, 1:2], cells[0:1, 0:1], AF.Exp),
         reads=[ck(0)], writes=[ck(1)])
    p.op("vector", lambda v: v.reciprocal(cells[0:1, 2:3], cells[0:1, 1:2]),
         reads=[ck(1)], writes=[ck(2)])
    p.op("scalar", lambda s: s.mul(cells[0:1, 3:4], cells[0:1, 2:3], 2.0),
         reads=[ck(2)], writes=[ck(3)])

    onr = m.sb(f"{pr}onr2", [1, 128])
    p.op("gpsimd", lambda g: g.memset(onr[:, :], 1.0), writes=[pr + "onr2"])

    vct = m.sb(f"{pr}vct", [64, 128])
    bp = m.sb(f"{pr}bp", [64, 1])
    p.dma("qpool", vct[:, 0:64], io["l0b_VcTRe"], writes=[pr + "vctR"])
    p.dma("qpool", vct[:, 64:128], io["l0b_VcTIm"], writes=[pr + "vctI"])
    p.dma("qpool", bp[:, :], io["l0b_Bp"], writes=[pr + "bp"])
    vr = m.sb(f"{pr}vr", [1, 24 * 64])
    VNAMES = ["BRe", "BIm", "CtRe", "CtIm", "pRe", "pIm", "qRe", "qIm",
              "LamRe", "LamIm", "LamReN", "LamImN",
              "v1R", "v1I", "v2R", "v2I", "v3R", "v3I", "v4R", "v4I",
              "v1IN", "v2IN", "v3IN", "v4IN"]
    VC = {n: vr[0:1, i * 64:(i + 1) * 64] for i, n in enumerate(VNAMES)}
    vk = {n: f"{pr}vec_{n}" for n in VC}
    for part, wk, rdk in ((0, "BRe", pr + "vctR"), (1, "BIm", pr + "vctI")):
        p.mm_group([lambda t, part=part: t.matmul(
            pool["P5"][0:1, 0:64], bp[:, :], vct[:, part * 64:(part + 1) * 64],
            start=True, stop=True)], reads=[pr + "bp", rdk], writes=["P5"])
        p.op("vector", lambda v, wk=wk: v.tensor_copy(VC[wk], pool["P5"][0:1, 0:64]),
             reads=["P5"], writes=[vk[wk]])
    for nm, src_ in [("Ct", "l0b_Ct"), ("p", "l0b_p"), ("q", "l0b_q"),
                     ("Lam", "l0b_Lam")]:
        p.dma("qpool", VC[nm + "Re"], io[src_][0:1, :], writes=[vk[nm + "Re"]],
              sem_key=vk[nm + "Re"])
        p.dma("qpool", VC[nm + "Im"], io[src_][1:2, :], writes=[vk[nm + "Im"]],
              sem_key=vk[nm + "Im"])
    p.op("vector", lambda v: v.tensor_scalar_mul(VC["LamReN"], VC["LamRe"], -1.0),
         reads=[vk["LamRe"]], writes=[vk["LamReN"]])
    p.op("vector", lambda v: v.tensor_scalar_mul(VC["LamImN"], VC["LamIm"], -1.0),
         reads=[vk["LamIm"]], writes=[vk["LamImN"]])
    vtmp = m.sb(f"{pr}vtmp", [1, 64])
    for (vn, an, bn) in [("v1", "Ct", "B"), ("v2", "Ct", "p"),
                         ("v3", "q", "B"), ("v4", "q", "p")]:
        cplx_mul(p, m, pr, VC[vn + "R"], VC[vn + "I"],
                 VC[an + "Re"], VC[an + "Im"], VC[bn + "Re"], VC[bn + "Im"],
                 keys=[vk[an + "Re"], vk[an + "Im"], vk[bn + "Re"], vk[bn + "Im"]],
                 okeys=[vk[vn + "R"], vk[vn + "I"]], conj_a=True,
                 tmp=vtmp[:, :], tmpk=pr + "vtmp")
        p.op("vector", lambda v, vn=vn: v.tensor_scalar_mul(
            VC[vn + "IN"], VC[vn + "I"], -1.0),
            reads=[vk[vn + "I"]], writes=[vk[vn + "IN"]])

    reps = m.sb(f"{pr}reps", [128, 14 * 64])
    RP = {}
    rpk = {}
    rep_names = ["LamRe", "LamReN", "v1R", "v1I", "v1IN", "v2R", "v2I", "v2IN",
                 "v3R", "v3I", "v3IN", "v4R", "v4I", "v4IN"]
    rep_names[0] = "LamIm"
    for i, n in enumerate(rep_names):
        sl = reps[:, i * 64:(i + 1) * 64]
        RP[n] = sl
        rpk[n] = f"{pr}rep_{n}"
        p.mm_group([lambda t, n=n, sl=sl: t.matmul(
            pool["P5"][:, 0:64], onr[:, :], VC[n], start=True, stop=True)],
            reads=[pr + "onr2", vk[n]], writes=["P5"])
        p.op("vector", lambda v, sl=sl: v.tensor_copy(sl, pool["P5"][:, 0:64]),
             reads=["P5"], writes=[rpk[n]])

    kc = m.sb(f"{pr}kc", [128, 4 * 32])
    KCn = ["tanth2", "gIm", "cRe", "cIm"]
    KC = {n: kc[:, i * 32:(i + 1) * 32] for i, n in enumerate(KCn)}
    kck = {n: f"{pr}kc_{n}" for n in KCn}
    p.dma("qpool", KC["tanth2"], io["l0b_tanth2"], writes=[kck["tanth2"]])
    p.dma("qpool", KC["cRe"], io["l0b_cRe"], writes=[kck["cRe"]])
    p.dma("qpool", KC["cIm"], io["l0b_cIm"], writes=[kck["cIm"]])
    s2col = m.sb(f"{pr}s2col", [128, 1])
    p.mm_group([lambda t: t.matmul(pool["P5"][:, 0:1], onr[:, :], cells[0:1, 3:4],
                                   start=True, stop=True)],
               reads=[pr + "onr2", ck(3)], writes=["P5"])
    p.op("vector", lambda v: v.tensor_copy(s2col[:, :], pool["P5"][:, 0:1]),
         reads=["P5"], writes=[pr + "s2col"])
    p.op("vector", lambda v: v.tensor_scalar_mul(KC["gIm"], KC["tanth2"],
         s2col[:, 0:1]), reads=[kck["tanth2"], pr + "s2col"], writes=[kck["gIm"]])

    dn = m.sb(f"{pr}dn", [128, 4 * 2048])
    DN = {n: dn[:, i * 2048:(i + 1) * 2048] for i, n in enumerate(
        ["b", "den", "rr", "ri"])}
    DN3 = {n: DN[n].rearrange("p (a b) -> p a b", a=32) for n in DN}
    dnk = {n: f"{pr}dn_{n}" for n in DN}
    gImB = KC["gIm"].unsqueeze(2).broadcast_to([128, 32, 64])
    LamImB = RP["LamIm"].unsqueeze(1).broadcast_to([128, 32, 64])
    aB = RP["LamReN"].unsqueeze(1).broadcast_to([128, 32, 64])
    p.op("vector", lambda v: v.tensor_tensor(DN3["b"], gImB, LamImB, ALU.subtract),
         reads=[kck["gIm"], rpk["LamIm"]], writes=[dnk["b"]])
    p.op("vector", lambda v: v.tensor_tensor(DN3["den"], aB, aB, ALU.mult),
         reads=[rpk["LamReN"]], writes=[dnk["den"]])
    p.op("vector", lambda v: v.tensor_mul(DN["rr"], DN["b"], DN["b"]),
         reads=[dnk["b"]], writes=[dnk["rr"]])
    p.op("vector", lambda v: v.tensor_add(DN["den"], DN["den"], DN["rr"]),
         reads=[dnk["den"], dnk["rr"]], writes=[dnk["den"]])
    p.op("vector", lambda v: v.reciprocal(DN["den"], DN["den"]),
         reads=[dnk["den"]], writes=[dnk["den"]])
    p.op("vector", lambda v: v.tensor_tensor(DN3["rr"], aB,
         DN3["den"], ALU.mult), reads=[rpk["LamReN"], dnk["den"]],
         writes=[dnk["rr"]])
    p.op("vector", lambda v: v.tensor_mul(DN["ri"], DN["b"], DN["den"]),
         reads=[dnk["b"], dnk["den"]], writes=[dnk["ri"]])
    p.op("vector", lambda v: v.tensor_scalar_mul(DN["ri"], DN["ri"], -1.0),
         reads=[dnk["ri"]], writes=[dnk["ri"]])

    kq = m.sb(f"{pr}kq", [128, 8 * 32])
    KR = {}
    krk_ = {}
    CT1 = DN["b"]
    CT2 = DN["den"]
    for qi, vn in enumerate(["v1", "v2", "v3", "v4"]):
        base = ["k00", "k01", "k10", "k11"][qi]
        for part in ("R", "I"):
            n = base + part
            sl = kq[:, (qi * 2 + (part == "I")) * 32:
                    (qi * 2 + (part == "I") + 1) * 32]
            KR[n] = sl
            krk_[n] = f"{pr}kq_{n}"
            if part == "R":
                m1, m2 = (DN3["rr"], vn + "R"), (DN3["ri"], vn + "IN")
            else:
                m1, m2 = (DN3["ri"], vn + "R"), (DN3["rr"], vn + "I")
            vb1 = RP[m1[1]].unsqueeze(1).broadcast_to([128, 32, 64])
            vb2 = RP[m2[1]].unsqueeze(1).broadcast_to([128, 32, 64])
            p.op("vector", lambda v, m1=m1, vb1=vb1: v.tensor_tensor(
                CT1.rearrange("p (a b) -> p a b", a=32), m1[0], vb1, ALU.mult),
                reads=[dnk["rr"], dnk["ri"], rpk[m1[1]]], writes=[dnk["b"]])
            p.op("vector", lambda v, m2=m2, vb2=vb2: v.tensor_tensor(
                CT2.rearrange("p (a b) -> p a b", a=32), m2[0], vb2, ALU.mult),
                reads=[dnk["rr"], dnk["ri"], rpk[m2[1]]], writes=[dnk["den"]])
            p.op("vector", lambda v: v.tensor_add(CT1, CT1, CT2),
                 reads=[dnk["b"], dnk["den"]], writes=[dnk["b"]])
            p.op("vector", lambda v, sl=sl: v.reduce_sum(
                sl.rearrange("p (a one) -> p a one", one=1),
                CT1.rearrange("p (a b) -> p a b", a=32), axis=AX.X),
                reads=[dnk["b"]], writes=[krk_[n]])

    at = m.sb(f"{pr}at", [128, 10 * 32])
    ATn = ["d1", "invR", "invI", "tR", "tI", "t2R", "t2I", "arR", "arI", "tmp"]
    AT = {n: at[:, i * 32:(i + 1) * 32] for i, n in enumerate(ATn)}
    ak = lambda n: f"{pr}at_{n}"
    p.op("scalar", lambda s: s.add(AT["d1"], KR["k11R"], 1.0),
         reads=[krk_["k11R"]], writes=[ak("d1")])
    p.op("vector", lambda v: v.tensor_mul(AT["invR"], AT["d1"], AT["d1"]),
         reads=[ak("d1")], writes=[ak("invR")])
    p.op("vector", lambda v: v.tensor_mul(AT["invI"], KR["k11I"], KR["k11I"]),
         reads=[krk_["k11I"]], writes=[ak("invI")])
    p.op("vector", lambda v: v.tensor_add(AT["invR"], AT["invR"], AT["invI"]),
         reads=[ak("invR"), ak("invI")], writes=[ak("invR")])
    p.op("vector", lambda v: v.reciprocal(AT["invR"], AT["invR"]),
         reads=[ak("invR")], writes=[ak("invR")])
    p.op("vector", lambda v: v.tensor_mul(AT["d1"], AT["d1"], AT["invR"]),
         reads=[ak("d1"), ak("invR")], writes=[ak("d1")])
    p.op("vector", lambda v: v.tensor_mul(AT["invI"], KR["k11I"], AT["invR"]),
         reads=[krk_["k11I"], ak("invR")], writes=[ak("invI")])
    p.op("vector", lambda v: v.tensor_scalar_mul(AT["invI"], AT["invI"], -1.0),
         reads=[ak("invI")], writes=[ak("invI")])
    cplx_mul(p, m, pr, AT["tR"], AT["tI"], KR["k01R"], KR["k01I"], AT["d1"],
             AT["invI"], keys=[krk_["k01R"], krk_["k01I"], ak("d1"), ak("invI")],
             okeys=[ak("tR"), ak("tI")], tmp=AT["tmp"], tmpk=ak("tmp"))
    cplx_mul(p, m, pr, AT["t2R"], AT["t2I"], AT["tR"], AT["tI"], KR["k10R"],
             KR["k10I"], keys=[ak("tR"), ak("tI"), krk_["k10R"], krk_["k10I"]],
             okeys=[ak("t2R"), ak("t2I")], tmp=AT["tmp"], tmpk=ak("tmp"))
    p.op("vector", lambda v: v.tensor_sub(AT["tR"], KR["k00R"], AT["t2R"]),
         reads=[krk_["k00R"], ak("t2R")], writes=[ak("tR")])
    p.op("vector", lambda v: v.tensor_sub(AT["tI"], KR["k00I"], AT["t2I"]),
         reads=[krk_["k00I"], ak("t2I")], writes=[ak("tI")])
    cplx_mul(p, m, pr, AT["arR"], AT["arI"], KC["cRe"], KC["cIm"], AT["tR"],
             AT["tI"], keys=[kck["cRe"], kck["cIm"], ak("tR"), ak("tI")],
             okeys=[ak("arR"), ak("arI")], tmp=AT["tmp"], tmpk=ak("tmp"))

    ar_dram = nc.dram_tensor(pr + "ar_dram", [2, 4096], F32).ap()
    p.dma("qscalar", ar_dram[0, :].rearrange("(a b) -> a b", a=128), AT["arR"],
          reads=[ak("arR")], writes=[pr + "ardbuf0"], sem_key=pr + "ard0")
    p.dma("qscalar", ar_dram[1, :].rearrange("(a b) -> a b", a=128), AT["arI"],
          reads=[ak("arI")], writes=[pr + "ardbuf1"], sem_key=pr + "ard1")
    art = m.sb(f"{pr}art", [64, 3 * 64])
    p.dma("qpool", art[:, 0:64], ar_dram[0, :].rearrange("(a b) -> a b", a=64),
          reads=[pr + "ardbuf0"], writes=[pr + "artR"], sem_key=pr + "artR")
    p.dma("qpool", art[:, 64:128], ar_dram[1, :].rearrange("(a b) -> a b", a=64),
          reads=[pr + "ardbuf1"], writes=[pr + "artI"], sem_key=pr + "artI")
    p.op("vector", lambda v: v.tensor_scalar_mul(art[:, 128:192], art[:, 64:128],
         -1.0), reads=[pr + "artI"], writes=[pr + "artIN"])

    dftc = m.sb(f"{pr}dftc", [64, 6 * 64])
    DF = {n: dftc[0:64, i * 64:(i + 1) * 64] for i, n in enumerate(
        ["ERe", "EIm", "twRe", "twIm", "VRe", "VIm"])}
    p.dma("qpool", dftc[:, 0:384], io["l0b_dft"], writes=[pr + "dft"])
    gt = m.sb(f"{pr}gt", [64, 6 * 64])
    GT = {n: gt[0:64, i * 64:(i + 1) * 64] for i, n in enumerate(
        ["GRe", "GIm", "GpRe", "GpIm", "GpImN", "KmS"])}
    p.mm_group([
        lambda t: t.matmul(pool["P4"][0:64, 0:64], art[:, 0:64], DF["ERe"],
                           start=True, stop=False),
        lambda t: t.matmul(pool["P4"][0:64, 0:64], art[:, 128:192], DF["EIm"],
                           start=False, stop=True)],
        reads=[pr + "artR", pr + "artIN", pr + "dft"], writes=["P4"])
    p.op("vector", lambda v: v.tensor_copy(GT["GRe"], pool["P4"][0:64, 0:64]),
         reads=["P4"], writes=[pr + "GRe"])
    p.mm_group([
        lambda t: t.matmul(pool["P4"][0:64, 0:64], art[:, 0:64], DF["EIm"],
                           start=True, stop=False),
        lambda t: t.matmul(pool["P4"][0:64, 0:64], art[:, 64:128], DF["ERe"],
                           start=False, stop=True)],
        reads=[pr + "artR", pr + "artI", pr + "dft"], writes=["P4"])
    p.op("vector", lambda v: v.tensor_copy(GT["GIm"], pool["P4"][0:64, 0:64]),
         reads=["P4"], writes=[pr + "GIm"])
    gtmp = m.sb(f"{pr}gtmp", [64, 64])
    cplx_mul(p, m, pr, GT["GpRe"], GT["GpIm"], DF["twRe"], DF["twIm"],
             GT["GRe"], GT["GIm"], keys=[pr + "dft", pr + "GRe", pr + "GIm"],
             okeys=[pr + "GpRe", pr + "GpIm"], tmp=gtmp[:, :], tmpk=pr + "gtmp")
    p.op("vector", lambda v: v.tensor_scalar_mul(GT["GpImN"], GT["GpIm"], -1.0),
         reads=[pr + "GpIm"], writes=[pr + "GpImN"])
    p.mm_group([
        lambda t: t.matmul(pool["P4"][0:64, 0:64], DF["VRe"], GT["GpRe"],
                           start=True, stop=False),
        lambda t: t.matmul(pool["P4"][0:64, 0:64], DF["VIm"], GT["GpImN"],
                           start=False, stop=True)],
        reads=[pr + "GpRe", pr + "GpImN", pr + "dft"], writes=["P4"])
    identf = m.sb(f"{pr}idf", [128, 128])
    p.dma("qpool", identf[:, :], io["identb"], writes=[pr + "idf"],
          sem_key=pr + "idf")
    kmt = m.sb(f"{pr}kmt", [64, 64])
    p.op("vector", lambda v: v.tensor_copy(kmt[:, :], pool["P4"][0:64, 0:64]),
         reads=["P4"], writes=[pr + "kmt"])
    kpad = nc.dram_tensor(pr + "kpad", [4224], F32).ap()
    zr = m.sb(f"{pr}zr", [1, 128])
    p.op("gpsimd", lambda g: g.memset(zr[:, :], 0.0), writes=[pr + "zr"])
    p.dma("qscalar", AP(kpad.tensor, 0, [[128, 1], [1, 128]]), zr[0:1, :],
          reads=[pr + "zr"], writes=[pr + "kpbuf0"], sem_key=pr + "kp0")
    p.dma("qscalar", AP(kpad.tensor, 128, [[64, 64], [1, 64]]), kmt[:, :],
          reads=[pr + "kmt"], writes=[pr + "kpbuf1"], sem_key=pr + "kp1")

    jb = m.sb(f"{pr}jb", [128, 128], BF16)
    p.dma("qpool", jb[:, :], io["l0b_J"], writes=[pr + "jb"], sem_key=pr + "jb")
    hk = m.sb(f"{pr}hk", [128, 128])
    hkb = m.sb(f"{pr}hkb", [128, 128], BF16)
    gsb = m.sb(f"{pr}gsb", [128, 32 * 128], BF16)
    dcol = m.sb(f"{pr}dcol", [128, 1])
    dtile = m.sb(f"{pr}dtile", [128, 128], BF16)
    dvc = m.sb(f"{pr}dvc", [1, 1])
    p.dma("qpool", dvc[:, :], io["l0b_D"], writes=[pr + "dvc"])
    p.mm_group([lambda t: t.matmul(pool["P5"][:, 0:1], onr[:, :], dvc[:, :],
                                   start=True, stop=True)],
               reads=[pr + "onr2", pr + "dvc"], writes=["P5"])
    p.op("vector", lambda v: v.tensor_copy(dcol[:, :], pool["P5"][:, 0:1]),
         reads=["P5"], writes=[pr + "dcol"])
    p.op("vector", lambda v: v.tensor_scalar_mul(dtile[:, :], identf[:, :],
         dcol[:, 0:1]), reads=[pr + "idf", pr + "dcol"], writes=[pr + "dtile"])
    for d in range(32):
        p.dma("qsync", hk[:, :], AP(kpad.tensor, 128 * d + 1, [[1, 128], [1, 128]]),
              reads=[pr + "kpbuf0", pr + "kpbuf1"], writes=[pr + "hk"],
              sem_key=pr + "hk")
        p.op("vector", lambda v: v.tensor_copy(hkb[:, :], hk[:, :]),
             reads=[pr + "hk"], writes=[pr + "hkb"])
        p.mm_group([lambda t: t.matmul(pool["P5"][:, 0:128], jb[:, :], hkb[:, :],
                                       start=True, stop=True)],
                   reads=[pr + "jb", pr + "hkb"], writes=["P5"])
        gk = f"{pr}g:{d}"
        p.op("vector", lambda v, d=d: v.tensor_copy(
            gsb[:, 128 * d:128 * (d + 1)], pool["P5"][:, 0:128]),
            reads=["P5"], writes=[gk])
        if d == 0:
            p.op("vector", lambda v, d=d: v.tensor_add(
                gsb[:, 128 * d:128 * (d + 1)], gsb[:, 128 * d:128 * (d + 1)],
                dtile[:, :]), reads=[gk, pr + "dtile"], writes=[gk])

    xo = [m.sb(f"{pr}xo{i}", [128, ntok]) for i in range(2)]
    gkeys = [f"{pr}g:{d}" for d in range(32)]
    if truncate == "kgen":
        for I in range(KT):
            xb, xbk = xo[I % 2], f"{pr}xo{I % 2}"
            p.dma("qsync", xb[:, :], io["xin"][128 * I:128 * (I + 1), :],
                  writes=[xbk], sem_key=xbk)
            p.dma("qscalar", io["xout"][128 * I:128 * (I + 1), :], xb[:, :],
                  reads=[xbk] + (gkeys if I == 0 else []), sem_key=xbk + "st")
        return
    for I in range(KT):
        xb, xbk = xo[I % 2], f"{pr}xo{I % 2}"
        p.dma("qsync", xb[:, :], io["xin"][128 * I:128 * (I + 1), :],
              writes=[xbk], sem_key=xbk)
        for c, cs in enumerate(cs_l):
            w = cs.stop - cs.start
            px, pxk = pool[f"P{c % 2}"], f"P{c % 2}"
            mms = []
            for J in range(I + 1):
                mms.append(lambda t, I=I, J=J, cs=cs, w=w, px=px: t.matmul(
                    px[:, 0:w], gsb[:, 128 * (I - J):128 * (I - J + 1)],
                    ub[:, J * ntok:(J + 1) * ntok][:, cs],
                    start=(J == 0), stop=(J == I)))
            p.mm_group(mms, reads=gkeys[:I + 1] + ubkeys[:I + 1], writes=[pxk])
            p.op("vector", lambda v, cs=cs, w=w, px=px, xb=xb:
                 v.scalar_tensor_tensor(xb[:, cs], px[:, 0:w], 1.0, xb[:, cs],
                                        ALU.mult, ALU.add),
                 reads=[pxk, xbk], writes=[xbk])
        p.dma("qscalar", io["xout"][128 * I:128 * (I + 1), :], xb[:, :],
              reads=[xbk], sem_key=xbk + "st")


def prep_l0ab(inp):
    ios = {}
    L = 4096
    kkk = np.arange(L)
    theta = 2 * np.pi * kkk / L
    tanth2 = np.tan(theta / 2)
    tanth2[L // 2] = 1.633123935319537e16
    Om = np.exp(-1j * theta)
    c = 2.0 / (1.0 + Om)
    ios["l0b_tanth2"] = tanth2.astype(np.float32).reshape(128, 32)
    ios["l0b_cRe"] = c.real.astype(np.float32).reshape(128, 32)
    ios["l0b_cIm"] = c.imag.astype(np.float32).reshape(128, 32)
    k64 = np.arange(64)
    E = np.exp(2j * np.pi * np.outer(k64, k64) / 64)
    tw = np.exp(2j * np.pi * np.outer(k64, k64) / 4096) / 4096
    V = E
    dft = np.concatenate([E.real, E.imag, tw.real, tw.imag, V.real, V.imag],
                         axis=1).astype(np.float32)
    ios["l0b_dft"] = np.ascontiguousarray(dft)
    ios["l0b_VcTRe"] = np.ascontiguousarray(inp["L0_attn_b_Vc_ri"][:, :, 0].T
                                            .astype(np.float32))
    ios["l0b_VcTIm"] = np.ascontiguousarray(inp["L0_attn_b_Vc_ri"][:, :, 1].T
                                            .astype(np.float32))
    ios["l0b_Bp"] = inp["L0_attn_b_Bp"].astype(np.float32)
    ios["l0b_Ct"] = np.ascontiguousarray(
        inp["L0_attn_b_Ct_ri"].reshape(64, 2).T.astype(np.float32))
    ios["l0b_p"] = np.ascontiguousarray(inp["L0_attn_b_p_ri"].T.astype(np.float32))
    ios["l0b_q"] = np.ascontiguousarray(inp["L0_attn_b_q_ri"].T.astype(np.float32))
    ios["l0b_Lam"] = np.ascontiguousarray(
        inp["L0_attn_b_Lambda_ri"].T.astype(np.float32))
    ios["l0b_log_step"] = inp["L0_attn_b_log_step"].reshape(1, 1).astype(np.float32)
    ios["l0b_D"] = inp["L0_attn_b_D"].reshape(1, 1).astype(np.float32)
    J = np.zeros((128, 128), np.float32)
    J[np.arange(128), 127 - np.arange(128)] = 1.0
    ios["l0b_J"] = J
    return ios


STAGE_EMITTERS["l0ab"] = emit_l0ab
STAGE_EMITTERS["l0ab_kgen"] = lambda p, m, io, ntok, pool: emit_l0ab(
    p, m, io, ntok, pool, truncate="kgen")


def emit_select(p, m, io, ntok, pool, pr="X"):
    fl = m.sb(f"{pr}flag", [128, 2])
    sc = m.sb(f"{pr}sc", [128, ntok])
    sc2 = m.sb(f"{pr}sc2", [128, ntok])
    p.dma("qpool", fl[:, :], io["slflag"], writes=[pr + "fl"])
    for o in range(KT):
        a, b = sc, sc2
        ka, kb2 = f"{pr}a", f"{pr}b"
        p.dma("qsync", a[:, :], io["selA"][128 * o:128 * (o + 1), :],
              writes=[ka], sem_key=ka)
        p.op("vector", lambda v, a=a: v.tensor_scalar_mul(
            a[:, :], a[:, :], fl[:, 0:1]), reads=[ka, pr + "fl"], writes=[ka])
        p.dma("qscalar", b[:, :], io["selB"][128 * o:128 * (o + 1), :],
              writes=[kb2], sem_key=kb2)
        p.op("vector", lambda v, b=b: v.tensor_scalar_mul(
            b[:, :], b[:, :], fl[:, 1:2]), reads=[kb2, pr + "fl"], writes=[kb2])
        p.op("vector", lambda v, a=a, b=b: v.scalar_tensor_tensor(
            a[:, :], b[:, :], 1.0, a[:, :], ALU.mult, ALU.add),
            reads=[ka, kb2], writes=[ka])
        p.dma("qscalar", io["xout"][128 * o:128 * (o + 1), :], a[:, :],
              reads=[ka], writes=[f"{pr}o{o}"], sem_key=pr + "ost")


def build_fused_program(host_shapes, ntok=1024, taps=False):
    nc = bass.Bass("TRN2", target_bir_lowering=False, debug=False)
    ctx = ExitStack()
    S2 = 2 * ntok
    io = {}
    io["xin"] = nc.dram_tensor("xin", [D, S2], F32,
                               kind="ExternalInput").ap()
    io["xout"] = nc.dram_tensor("xout", [D, ntok], F32,
                                kind="ExternalOutput").ap()
    kind = "ExternalOutput" if taps else "Internal"
    xf = [io["xin"]]
    for i in range(1, 7):
        xf.append(nc.dram_tensor(f"xf{i}", [D, S2], F32, kind=kind).ap())
    x6own = nc.dram_tensor("x6own", [D, ntok], F32, kind=kind).ap()
    x7own = nc.dram_tensor("x7own", [D, ntok], F32, kind=kind).ap()
    for name, shape in host_shapes.items():
        io[name] = nc.dram_tensor(name, list(shape), F32,
                                  kind="ExternalInput").ap()
    sm = StackMem(nc, ctx)
    pool = make_pool(sm)
    p = Prog(nc, ctx)
    first = [True]

    marks = []

    def inst(fn, pr, xin, xout, extra=None):
        if not first[0]:
            p.barrier()
        first[0] = False
        sm.reset()
        sio = dict(io)
        sio["xin"] = xin
        sio["xout"] = xout
        sio.update(extra or {})
        fn(p, sm, sio, ntok, pool, pr=pr)
        marks.append((pr, p.n_ops["tensor"], p.n_ops["vector"],
                      p.n_ops["scalar"], p.n_dmas))
    nc._stage_marks = marks

    for i, (fn, pre) in enumerate([(emit_l0aa, "A"), (emit_l0ffn, "F"),
                                   (emit_l0ab, "S"), (emit_l0ffn, "E")]):
        for hh in range(2):
            cols = slice(hh * ntok, (hh + 1) * ntok)
            inst(fn, f"{pre}{hh}", xf[i][:, cols], xf[i + 1][:, cols])
    for hh in range(2):
        cols = slice(hh * ntok, (hh + 1) * ntok)
        inst(emit_l1aa, f"G{hh}", xf[4][:, cols], xf[5][:, cols],
             extra={"xkv": xf[4],
                    "l1a_km2": io[f"l1a_km2h{hh}"],
                    "l1a_bm": io[f"l1a_bmh{hh}"],
                    "l1a_qcos": io[f"l1a_qcosh{hh}"],
                    "l1a_qsin": io[f"l1a_qsinh{hh}"]})
    for hh in range(2):
        cols = slice(hh * ntok, (hh + 1) * ntok)
        inst(emit_l1ffn, f"Q{hh}", xf[5][:, cols], xf[6][:, cols])
    inst(emit_select, "X", xf[6], x6own,
         extra={"selA": xf[6][:, 0:ntok], "selB": xf[6][:, ntok:S2]})
    inst(emit_l1ab, "N", x6own, x7own, extra={"xkv": xf[6]})
    inst(emit_l1ffn, "Z", x7own, io["xout"])
    p.finish("sync")
    p.emit()
    ctx.close()
    return nc


def fused_host_arrays(inputs, ntok=1024):
    ident = const_identb()
    shared = {"identb": ident}
    shared.update(prep_l0aa(inputs))
    shared.update(prep_l0ffn(inputs))
    shared.update(prep_l0ab(inputs))
    shared.update(prep_l1ffn(inputs))
    HKEYS = ("l1a_km2", "l1a_bm", "l1a_qcos", "l1a_qsin")
    for h in (0, 1):
        d = prep_l1aa(inputs, h, ntok=ntok)
        d["l1a_cos"] = np.ascontiguousarray(d["l1a_cos"][:, :2 * ntok])
        d["l1a_sin"] = np.ascontiguousarray(d["l1a_sin"][:, :2 * ntok])
        for k in HKEYS:
            shared[f"{k}h{h}"] = d.pop(k)
        if h == 0:
            shared.update(d)
    byh = []
    for h in (0, 1):
        d = dict(prep_l1ab(inputs, h, ntok=ntok))
        fl = np.zeros((128, 2), np.float32)
        fl[:, h] = 1.0
        d["slflag"] = fl
        byh.append(d)
    return shared, byh


def kernel_fused(inputs):
    from concourse.bass_utils import run_bass_kernel_spmd
    x = np.asarray(inputs["L0_attn_a_hidden_states"], np.float32)
    B, S, Dm = x.shape
    ntok = 1024
    shared, byh = fused_host_arrays(inputs, ntok)
    host_shapes = {k: v.shape for k, v in shared.items()}
    host_shapes.update({k: v.shape for k, v in byh[0].items()})
    ncp = build_fused_program(host_shapes, ntok)
    xb = [np.ascontiguousarray(x[b].T) for b in range(B)]
    in_maps = []
    for c in range(8):
        b, h = c // 2, c % 2
        im = dict(shared)
        im.update(byh[h])
        im["xin"] = xb[b]
        in_maps.append(im)
    res = run_bass_kernel_spmd(ncp, in_maps, core_ids=list(range(8)))
    out = np.empty((B, S, Dm), np.float32)
    for c in range(8):
        b, h = c // 2, c % 2
        out[b, h * ntok:(h + 1) * ntok, :] = np.asarray(
            res.results[c]["xout"]).T
    return out


def _launch(stage, host, xin_list, cores, xkv_list=None):
    from concourse.bass_utils import run_bass_kernel_spmd
    percore = {"xkv": (D, 2048)} if xkv_list is not None else None
    nc = build_stage_program(stage, 1024, host, percore=percore)
    in_maps = []
    for i, c in enumerate(cores):
        im = dict(host)
        im["xin"] = np.ascontiguousarray(xin_list[i])
        if xkv_list is not None:
            im["xkv"] = np.ascontiguousarray(xkv_list[i])
        in_maps.append(im)
    res = run_bass_kernel_spmd(nc, in_maps, core_ids=cores)
    return [np.asarray(r["xout"]) for r in res.results]


def kernel(**inputs):
    inputs.get("L0_attn_a_wl"), inputs.get("L0_attn_a_bl")
    return kernel_fused(inputs)


def kernel_multilaunch(**inputs):
    x = np.asarray(inputs["L0_attn_a_hidden_states"], np.float32)
    B, S, Dm = x.shape
    xc = [np.ascontiguousarray(x[b].T[:, h * 1024:(h + 1) * 1024])
          for b in range(B) for h in range(2)]
    ident = const_identb()
    hostA = {"identb": ident}; hostA.update(prep_l0aa(inputs))
    hostF = {"identb": ident}; hostF.update(prep_l0ffn(inputs))
    hostB = {"identb": ident}; hostB.update(prep_l0ab(inputs))
    host1F = {"identb": ident}; host1F.update(prep_l1ffn(inputs))
    host1A, host1B = {}, {}
    for h in (0, 1):
        ha = {"identb": ident}; ha.update(prep_l1aa(inputs, h))
        ha["l1a_cos"] = np.ascontiguousarray(ha["l1a_cos"][:, :2048])
        ha["l1a_sin"] = np.ascontiguousarray(ha["l1a_sin"][:, :2048])
        host1A[h] = ha
        hb = {"identb": ident}; hb.update(prep_l1ab(inputs, h, ntok=1024))
        host1B[h] = hb

    allcores = list(range(8))

    def tp(stage, host):
        outs = _launch(stage, host, xc, allcores)
        for i in range(8):
            xc[i] = outs[i]

    def attn1(stage_base, hostmap):
        from concourse.bass_utils import run_bass_kernel_spmd
        xkv = [np.hstack([xc[2 * b], xc[2 * b + 1]]) for b in range(B)]
        ncp = build_stage_program(f"{stage_base}_h0", 1024, hostmap[0],
                                  percore={"xkv": (D, 2048)})
        in_maps = []
        for c in range(8):
            b, h = c // 2, c % 2
            im = dict(hostmap[h])
            im["xin"] = np.ascontiguousarray(xc[c])
            im["xkv"] = np.ascontiguousarray(xkv[b])
            in_maps.append(im)
        res = run_bass_kernel_spmd(ncp, in_maps, core_ids=allcores)
        for c in range(8):
            xc[c] = np.asarray(res.results[c]["xout"])

    tp("l0aa", hostA)
    tp("l0ffn", hostF)
    tp("l0ab", hostB)
    tp("l0ffn", hostF)
    attn1("l1aa", host1A)
    tp("l1ffn", host1F)
    attn1("l1ab", host1B)
    tp("l1ffn", host1F)

    out = np.empty((B, S, Dm), np.float32)
    for b in range(B):
        for h in range(2):
            out[b, h * 1024:(h + 1) * 1024, :] = xc[2 * b + h].T
    return out
